# version 14 (final)
# speedup vs baseline: 1.5088x; 1.2428x over previous
import math
from contextlib import ExitStack

import numpy as np

import concourse.bass as bass
import concourse.mybir as mybir
from concourse.bass_utils import run_bass_kernel_spmd

F32 = mybir.dt.float32
BF16 = mybir.dt.bfloat16
AF = mybir.ActivationFunctionType
ALU = mybir.AluOpType
AX = mybir.AxisListType

N_CORES = 8
USE_PROG2 = True
D_MODEL = 1024
T_SEQ = 2048
N_SEQ_PER_CORE = 2



ENGINES = ("tensor", "vector", "scalar", "gpsimd", "sync")


class Prog:

    NCHAIN = 8
    USE_CHAIN = True

    def __init__(self, nc, es):
        self.nc = nc
        self.es = es
        self.chain = ([es.enter_context(nc.semaphore(f"ch{i}")) for i in range(self.NCHAIN)]
                      if self.USE_CHAIN else [])
        self.ccount = [0] * self.NCHAIN
        self.nops = 0
        self.steps = {e: [] for e in ENGINES}
        self.slots = {}
        self.store_sem = es.enter_context(nc.semaphore("stsem"))
        self.nstores = 0

    def _prev_wait(self):
        if self.nops == 0:
            return None
        i = (self.nops - 1) % self.NCHAIN
        return (self.chain[i], self.ccount[i])

    def _cur_mark(self):
        return self._prev_wait()

    def gate_stores(self):
        self._gate_stores = True

    def comp(self, engine, fn, needs=(), wait_stores=False):
        if getattr(self, "_gate_stores", False):
            wait_stores = True
            self._gate_stores = False
        prev = self._prev_wait()
        i = self.nops % self.NCHAIN
        self.ccount[i] += 1
        inc = (self.chain[i], 1)
        slot_waits = []
        if wait_stores and self.nstores:
            slot_waits.append((self.store_sem, 16 * self.nstores))
        for s in needs:
            sl = self.slots[s]
            slot_waits.append((sl["sem"], 16 * sl["issued"]))
        self.steps[engine].append(("comp", prev, slot_waits, fn, inc, self._gseq()))
        self.nops += 1

    def load(self, slot, fn, queue="sync", gate_stores=False):
        if slot not in self.slots:
            sem = self.es.enter_context(self.nc.semaphore(f"ld_{slot}"))
            self.slots[slot] = {"sem": sem, "issued": 0, "mark": None, "queue": queue}
        sl = self.slots[slot]
        assert sl["queue"] == queue, f"slot {slot}: one DMA backend per semaphore"
        wait = sl["mark"]
        extra = []
        cur = self._cur_mark()
        if cur is not None and cur != wait:
            extra.append(cur)
        if gate_stores and self.nstores:
            extra.append((self.store_sem, 16 * self.nstores))
        sl["issued"] += 1
        self.steps[queue].append(("load", wait, extra, fn, (sl["sem"], 16), self._gseq()))

    def mark(self, slot):
        if slot in self.slots:
            self.slots[slot]["mark"] = self._cur_mark()

    def store(self, fn, queue="sync"):
        prev = self._prev_wait()
        self.nstores += 1
        self.steps[queue].append(("store", prev, [], fn, (self.store_sem, 16), self._gseq()))

    def _gseq(self):
        self._g = getattr(self, "_g", 0) + 1
        return self._g

    def tag(self, name):
        self._tag = name

    def finish(self):
        self.steps["sync"].append(("finwait", (self.store_sem, 16 * self.nstores), [], None, None, self._gseq()))

    def emit(self):
        with self.nc.Block() as block:
            for ename in ENGINES:
                seq = self.steps[ename]
                if not seq:
                    continue

                def make(seq=seq):
                    def _(eng):
                        for kind, wait, slot_waits, fn, inc, _g in seq:
                            if wait is not None:
                                eng.wait_ge(wait[0], wait[1])
                            for s, v in slot_waits:
                                if v > 0:
                                    eng.wait_ge(s, v)
                            if kind == "finwait":
                                continue
                            inst = fn(eng)
                            inst.then_inc(inc[0], inc[1])
                    return _

                getattr(block, ename)(make())



def _build_selftest(K=128, M=64, N=256):
    nc = bass.Bass("TRN2", target_bir_lowering=False)
    es = ExitStack()
    a = nc.dram_tensor("a", [K, M], F32, kind="ExternalInput").ap()
    b = nc.dram_tensor("b", [K, N], F32, kind="ExternalInput").ap()
    b2 = nc.dram_tensor("b2", [K, N], F32, kind="ExternalInput").ap()
    ident = nc.dram_tensor("ident", [128, 128], F32, kind="ExternalInput").ap()
    y = nc.dram_tensor("y", [M, N], F32, kind="ExternalOutput").ap()
    t_out = nc.dram_tensor("t_out", [M, 128], F32, kind="ExternalOutput").ap()
    w_bf = nc.dram_tensor("w_bf", [K, N], F32, kind="ExternalOutput").ap()

    sa = es.enter_context(nc.sbuf_tensor("sa", [K, M], BF16))
    sb = es.enter_context(nc.sbuf_tensor("sb", [K, N], BF16))
    sb2 = es.enter_context(nc.sbuf_tensor("sb2", [K, N], BF16))
    saf = es.enter_context(nc.sbuf_tensor("saf", [K, M], F32))
    sid = es.enter_context(nc.sbuf_tensor("sid", [128, 128], F32))
    sy = es.enter_context(nc.sbuf_tensor("sy", [M, N], F32))
    sto = es.enter_context(nc.sbuf_tensor("sto", [M, 128], F32))
    swb = es.enter_context(nc.sbuf_tensor("swb", [K, N], F32))
    py = es.enter_context(nc.psum_tensor("py", [M, N], F32))
    pt = es.enter_context(nc.psum_tensor("pt", [M, 128], F32))

    p = Prog(nc, es)
    p.load("a", lambda e: e.dma_start(out=sa[:], in_=a[:]), queue="gpsimd")
    p.load("af", lambda e: e.dma_start(out=saf[:], in_=a[:]), queue="sync")
    p.load("b", lambda e: e.dma_start(out=sb[:], in_=b[:]), queue="gpsimd")
    p.load("b", lambda e: e.dma_start(out=sb2[:], in_=b2[:]), queue="gpsimd")
    p.load("id", lambda e: e.dma_start(out=sid[:], in_=ident[:]), queue="sync")

    def mm_group(eng):
        eng.matmul(py[:], sa[:], sb[:], start=True, stop=False)
        return eng.matmul(py[:], sa[:], sb2[:], start=False, stop=True)

    p.comp("tensor", mm_group, needs=("a", "b"))
    p.comp("vector", lambda e: e.tensor_copy(out=sy[:], in_=py[:]))
    p.comp("tensor", lambda e: e.transpose(pt[0:M, :], saf[:, 0:M], sid[:]), needs=("id", "af"))
    p.comp("vector", lambda e: e.tensor_copy(out=sto[:], in_=pt[0:M, :]))
    p.comp("vector", lambda e: e.tensor_copy(out=swb[:], in_=sb[:]))

    p.store(lambda e: e.dma_start(out=y[:], in_=sy[:]))
    p.store(lambda e: e.dma_start(out=t_out[:], in_=sto[:]))
    p.store(lambda e: e.dma_start(out=w_bf[:], in_=swb[:]))
    p.finish()
    p.emit()
    nc.finalize()
    return nc, es




def ceil_div(a, b):
    return -(-a // b)


class Ctx:

    def __init__(self, nc, es, p):
        self.nc = nc
        self.es = es
        self.p = p
        self.psum = [es.enter_context(nc.psum_tensor(f"pb{i}", [128, 512], F32)) for i in range(7)]
        self.psbf = es.enter_context(nc.psum_tensor("psbf", [128, 1024], BF16))
        from collections import deque
        self._rr = deque(range(7))
        self.bufs = {}
        self.const_vals = {}

    def bank(self):
        i = self._rr.popleft()
        self._rr.append(i)
        return self.psum[i]

    def reserve(self):
        i = self._rr.popleft()
        return self.psum[i]

    def release(self, t):
        self._rr.append(self.psum.index(t))

    def constap(self, value, nparts=128):
        if value not in self.const_vals:
            assert len(self.const_vals) < 32
            self.const_vals[value] = len(self.const_vals)
        i = self.const_vals[value]
        return self.bufs["cpool"][0:nparts, i:i + 1]

    def banks(self, n):
        return [self.bank() for _ in range(n)]

    def sbuf(self, name, shape, dtype):
        scopes = getattr(self, "_scopes", None)
        target = scopes[-1] if scopes else self.es
        self._uniq = getattr(self, "_uniq", 0) + 1
        t = target.enter_context(self.nc.sbuf_tensor(f"{name}_u{self._uniq}", shape, dtype))
        self.bufs[name] = t
        return t

    def scope(self):
        from contextlib import contextmanager

        @contextmanager
        def _s():
            es = ExitStack()
            if not hasattr(self, "_scopes"):
                self._scopes = []
            self._scopes.append(es)
            try:
                yield
            finally:
                es.close()
                self._scopes.pop()
                if hasattr(self.p, "barrier"):
                    self.p.barrier()
        return _s()


def emit_transpose_in(ctx, x_tok_dram, xT_sb, T, nfeat_chunks=8, dtype_bytes=4):
    p, nc = ctx.p, ctx.nc
    stage_full = ctx.bufs["tstage"]
    ident = ctx.bufs["ident"]
    ntt = T // 128
    for tt in range(ntt):
        slot = f"tin{tt % 2}"
        stage = stage_full[:, (tt % 2) * 1024:(tt % 2) * 1024 + 1024]
        p.load(slot, lambda e, tt=tt, stage=stage: e.dma_start(
            out=stage[:], in_=x_tok_dram[tt * 128:(tt + 1) * 128, :]), queue="sync")
        for g in range(2):
            pb = ctx.bank()

            def tr(eng, g=g, tt=tt, stage=stage, pb=pb):
                last = None
                for i in range(4):
                    c = g * 4 + i
                    last = eng.matmul(pb[:, i * 128:(i + 1) * 128],
                                      stage[:, c * 128:(c + 1) * 128], ident[:],
                                      start=(i == 0), stop=(i == 3), is_transpose=True)
                return last

            p.comp("tensor", tr, needs=(slot,))

            def ev(eng, g=g, tt=tt, pb=pb):
                last = None
                for i in range(4):
                    c = g * 4 + i
                    last = eng.tensor_copy(
                        out=xT_sb[:, c * T + tt * 128: c * T + (tt + 1) * 128],
                        in_=pb[:, i * 128:(i + 1) * 128])
                return last

            p.comp("vector", ev)
        p.mark(slot)


def emit_transpose_out(ctx, xT_sb, x_tok_dram, T):
    p, nc = ctx.p, ctx.nc
    stage = ctx.bufs["tstage"][:, 0:1024]
    ident = ctx.bufs["ident"]
    ntt = T // 128
    for tt in range(ntt):
        for g in range(2):
            pb = ctx.bank()

            def tr(eng, g=g, tt=tt, pb=pb):
                last = None
                for i in range(4):
                    c = g * 4 + i
                    last = eng.matmul(pb[:, i * 128:(i + 1) * 128],
                                      xT_sb[:, c * T + tt * 128: c * T + (tt + 1) * 128],
                                      ident[:],
                                      start=(i == 0), stop=(i == 3), is_transpose=True)
                return last

            p.comp("tensor", tr)

            def ev(eng, g=g, pb=pb):
                last = None
                for i in range(4):
                    c = g * 4 + i
                    last = eng.tensor_copy(out=stage[:, c * 128:(c + 1) * 128],
                                           in_=pb[:, i * 128:(i + 1) * 128])
                return last

            p.comp("vector", ev, wait_stores=(g == 0))
        p.store(lambda e, tt=tt: e.dma_start(
            out=x_tok_dram[tt * 128:(tt + 1) * 128, :], in_=stage[:]))


def emit_ln_fm(ctx, xT_sb, T, lnb_sb, g_col=None, b_col=None, nchunks=8,
               stride=None, t0=0, lnb_stride=None, lnb_t0=0,
               xc_fn=None, lc_fn=None, cast_fn=None, m_out=None, r_out=None):
    if stride is None:
        stride = T
    if lnb_stride is None:
        lnb_stride = stride
    p, nc = ctx.p, ctx.nc
    ones_col = ctx.bufs["ones_col"]
    ones_row = ctx.bufs["ones_row"]
    st_sum, st_ssq = ctx.bufs["st_sum"], ctx.bufs["st_ssq"]
    st_m, st_e2, st_tmp, st_rstd = (ctx.bufs["st_m"], ctx.bufs["st_e2"],
                                    ctx.bufs["st_tmp"], ctx.bufs["st_rstd"])
    if m_out is None:
        m_out = st_m[:, 0:T]
    if r_out is None:
        r_out = st_rstd[:, 0:T]
    mb = ctx.bufs["mb"]
    rb = ctx.bufs["rb"]
    sqf = ctx.bufs["sqscr"]
    tscrf = ctx.bufs["tscr"]
    D = nchunks * 128
    ntt = T // 512

    xc = xc_fn if xc_fn is not None else (
        lambda c: xT_sb[:, c * stride + t0: c * stride + t0 + T])
    lc = lc_fn if lc_fn is not None else (
        lambda c: lnb_sb[:, c * lnb_stride + lnb_t0: c * lnb_stride + lnb_t0 + T])
    cast = cast_fn if cast_fn is not None else lc

    for c in range(nchunks):
        p.comp("scalar", lambda e, c=c: e.copy(out=cast(c), in_=xc(c)))
    for tt in range(ntt):
        psA, psB = ctx.bank(), ctx.bank()
        for c in range(nchunks):
            sl = (tt * 512, (tt + 1) * 512)
            sq = sqf[:, (c % 2) * 512:(c % 2) * 512 + 512]
            p.comp("scalar", lambda e, c=c, sl=sl, sq=sq: e.activation(
                out=sq, in_=cast(c)[:, sl[0]:sl[1]], func=AF.Square))
            p.comp("tensor", lambda e, c=c, sl=sl, psA=psA: e.matmul(
                psA[0:1, :], ones_col[:], cast(c)[:, sl[0]:sl[1]],
                start=(c == 0), stop=(c == nchunks - 1)))
            p.comp("tensor", lambda e, c=c, psB=psB, sq=sq: e.matmul(
                psB[0:1, :], ones_col[:], sq,
                start=(c == 0), stop=(c == nchunks - 1)))
        p.comp("vector", lambda e, tt=tt, psA=psA: e.tensor_copy(
            out=st_sum[:, tt * 512:(tt + 1) * 512], in_=psA[0:1, :]))
        p.comp("vector", lambda e, tt=tt, psB=psB: e.tensor_copy(
            out=st_ssq[:, tt * 512:(tt + 1) * 512], in_=psB[0:1, :]))
    p.comp("scalar", lambda e: e.mul(out=m_out, in_=st_sum[:, 0:T], mul=1.0 / D))
    p.comp("scalar", lambda e: e.mul(out=st_e2[:, 0:T], in_=st_ssq[:, 0:T], mul=1.0 / D))
    p.comp("vector", lambda e: e.tensor_mul(out=st_tmp[:, 0:T], in0=m_out, in1=m_out))
    p.comp("vector", lambda e: e.tensor_sub(out=st_tmp[:, 0:T], in0=st_e2[:, 0:T], in1=st_tmp[:, 0:T]))
    eps_ap = ctx.constap(1e-5, 1)
    p.comp("scalar", lambda e: e.activation(out=st_tmp[:, 0:T], in_=st_tmp[:, 0:T], func=AF.Ln, bias=eps_ap))
    p.comp("scalar", lambda e: e.activation(out=r_out, in_=st_tmp[:, 0:T], func=AF.Exp, scale=-0.5))
    for tt in range(ntt):
        pm, pr = ctx.bank(), ctx.bank()
        p.comp("tensor", lambda e, tt=tt, pm=pm: e.matmul(
            pm[:, :], ones_row[0:1, :], m_out[:, tt * 512:(tt + 1) * 512], start=True, stop=True))
        p.comp("tensor", lambda e, tt=tt, pr=pr: e.matmul(
            pr[:, :], ones_row[0:1, :], r_out[:, tt * 512:(tt + 1) * 512], start=True, stop=True))
        p.comp("vector", lambda e, tt=tt, pm=pm: e.tensor_copy(
            out=mb[:, tt * 512:(tt + 1) * 512], in_=pm[:, :]))
        p.comp("vector", lambda e, tt=tt, pr=pr: e.tensor_copy(
            out=rb[:, tt * 512:(tt + 1) * 512], in_=pr[:, :]))
    for c in range(nchunks):
        tscr = tscrf[:, (c % 2) * T:(c % 2) * T + T]
        p.comp("vector", lambda e, c=c, tscr=tscr: e.tensor_sub(out=tscr, in0=xc(c), in1=mb[:, 0:T]))
        if g_col is None:
            p.comp("vector", lambda e, c=c, tscr=tscr: e.tensor_mul(out=lc(c), in0=tscr, in1=rb[:, 0:T]))
        else:
            p.comp("vector", lambda e, c=c, tscr=tscr: e.tensor_mul(out=tscr, in0=tscr, in1=rb[:, 0:T]))
            p.comp("vector", lambda e, c=c, tscr=tscr: e.tensor_scalar(
                out=lc(c), in0=tscr, scalar1=g_col(c), scalar2=b_col(c),
                op0=ALU.mult, op1=ALU.add))


def emit_proj_fm(ctx, wslot_fn, lnb_sb, T, nin_chunks=8,
                 out_cols=128, lnb_col0=0, evac=None):
    p = ctx.p
    ntt = T // 512
    for tt in range(ntt):
        pb = ctx.bank()
        slots = set()
        waps = []
        for mc in range(nin_chunks):
            slot, wap = wslot_fn(mc)
            if slot:
                slots.add(slot)
            waps.append(wap)

        def mm(eng, tt=tt, pb=pb, waps=waps):
            last = None
            for mc in range(nin_chunks):
                last = eng.matmul(
                    pb[:, :], waps[mc],
                    lnb_sb[:, lnb_col0 + mc * T + tt * 512: lnb_col0 + mc * T + tt * 512 + 512],
                    start=(mc == 0), stop=(mc == nin_chunks - 1))
            return last

        p.comp("tensor", mm, needs=tuple(slots))
        evac(tt, pb)


def emit_l0_attention(ctx, T, lnb_sb, xT_sb):
    p, nc = ctx.p, ctx.nc
    qT = ctx.bufs["qT"]
    kT = ctx.bufs["kT"]
    v0 = ctx.bufs["v0"]
    v1 = ctx.bufs["v1"]
    opair = ctx.bufs["opair"]
    expsf = ctx.bufs["expscr"]
    rlf = ctx.bufs["rlscr"]
    wqf = ctx.bufs["wq3"]
    wof = ctx.bufs["wo"]
    ones_row = ctx.bufs["ones_row"]
    masks = ctx.bufs["cmask"]
    dram = ctx.dram
    ws = getattr(ctx, "wsrc", dram)
    wqueue = getattr(ctx, "wqueue", lambda: "gpsimd")

    ntt128 = T // 128
    nqb = T // 512

    for pc in range(8):
        wslot = "wqkv"
        wq = wqf[:, (pc % 2) * 3072:(pc % 2) * 3072 + 3072]
        for j, wname in enumerate(("WqT", "WkT", "WvT")):
            src = ws[wname].rearrange("(c p) n -> p c n", p=128)[:, :, pc * 128:(pc + 1) * 128]
            dst = wq[:, j * 1024:(j + 1) * 1024].rearrange("p (c n) -> p c n", c=8)
            p.load(wslot, lambda e, a=dst, s_=src: e.dma_start(out=a, in_=s_),
                   queue=wqueue())

        def q_evac(tt, pb):
            p.comp("vector", lambda e, tt=tt, pb=pb: e.tensor_scalar(
                out=qT[:, tt * 512:(tt + 1) * 512], in0=pb[:, :],
                scalar1=0.125, scalar2=None, op0=ALU.mult))

        emit_proj_fm(ctx, lambda mc: (wslot, wq[:, 0 * 1024 + mc * 128: 0 * 1024 + mc * 128 + 128]),
                     lnb_sb, T, evac=q_evac)

        def k_evac(tt, pb):
            p.comp("vector", lambda e, tt=tt, pb=pb: e.tensor_copy(
                out=kT[:, tt * 512:(tt + 1) * 512], in_=pb[:, :]))

        emit_proj_fm(ctx, lambda mc: (wslot, wq[:, 1 * 1024 + mc * 128: 1 * 1024 + mc * 128 + 128]),
                     lnb_sb, T, evac=k_evac)

        for tt in range(ntt128):
            pb = ctx.bank()

            def vmm(eng, tt=tt, pb=pb, wq=wq):
                last = None
                for mc in range(8):
                    last = eng.matmul(
                        pb[:, 0:128],
                        lnb_sb[:, mc * T + tt * 128: mc * T + (tt + 1) * 128],
                        wq[:, 2 * 1024 + mc * 128: 2 * 1024 + (mc + 1) * 128],
                        start=(mc == 0), stop=(mc == 7))
                return last

            p.comp("tensor", vmm, needs=(wslot,))
            p.comp("vector", lambda e, tt=tt, pb=pb: e.tensor_copy(
                out=v0[:, tt * 65: tt * 65 + 64], in_=pb[:, 0:64]))
            p.comp("vector", lambda e, tt=tt, pb=pb: e.tensor_copy(
                out=v1[:, tt * 65: tt * 65 + 64], in_=pb[:, 64:128]))
        p.mark(wslot)

        for hh in range(2):
            hofs = hh * 64
            vt = v0 if hh == 0 else v1
            for qb in range(nqb):
                po = ctx.reserve()
                rl = rlf[:, (qb % 2) * 512:(qb % 2) * 512 + 512]
                nkc = 4 * qb + 4
                for kc in range(nkc):
                    ps = ctx.bank()
                    exps = expsf[:, (kc % 4) * 512:(kc % 4) * 512 + 512]
                    p.comp("tensor", lambda e, kc=kc, qb=qb, hofs=hofs, ps=ps: e.matmul(
                        ps[:, :],
                        kT[hofs:hofs + 64, kc * 128:(kc + 1) * 128],
                        qT[hofs:hofs + 64, qb * 512:(qb + 1) * 512],
                        start=True, stop=True))
                    nb20 = ctx.constap(-20.0, 128)
                    p.comp("scalar", lambda e, ps=ps, nb20=nb20, exps=exps: e.activation(
                        out=exps[:, :], in_=ps[:, :], func=AF.Exp, bias=nb20))
                    r = kc - 4 * qb
                    if r >= 0:
                        p.comp("vector", lambda e, r=r, exps=exps: e.tensor_mul(
                            out=exps[:, :], in0=exps[:, :],
                            in1=masks[:, r * 512:(r + 1) * 512]))
                    p.comp("tensor", lambda e, kc=kc, po=po, vt=vt, nkc=nkc, exps=exps: e.matmul(
                        po[0:65, :], vt[:, kc * 65:(kc + 1) * 65], exps[:, :],
                        start=(kc == 0), stop=(kc == nkc - 1)))
                ctx.release(po)
                p.comp("scalar", lambda e, po=po, rl=rl: e.activation(out=rl[0:1, 0:512], in_=po[64:65, :],
                                                                      func=AF.Ln))
                p.comp("scalar", lambda e, rl=rl: e.activation(out=rl[0:1, 0:512], in_=rl[0:1, 0:512],
                                                               func=AF.Exp, scale=-1.0))
                pb2 = ctx.bank()
                p.comp("tensor", lambda e, pb2=pb2, rl=rl: e.matmul(
                    pb2[0:64, :], ones_row[0:1, 0:64], rl[0:1, 0:512], start=True, stop=True))
                p.comp("vector", lambda e, pb2=pb2, rl=rl: e.tensor_copy(out=rl[0:64, :], in_=pb2[0:64, :]))
                p.comp("vector", lambda e, po=po, qb=qb, pc=pc, hofs=hofs, rl=rl: e.tensor_mul(
                    out=opair[hofs:hofs + 64, pc * T + qb * 512: pc * T + qb * 512 + 512],
                    in0=po[0:64, :], in1=rl[0:64, :]))

    for nc_ in range(8):
        wo = wof[:, (nc_ % 2) * 1024:(nc_ % 2) * 1024 + 1024]
        src_wo = ws["WoT"].rearrange("(c p) n -> p c n", p=128)[:, :, nc_ * 128:(nc_ + 1) * 128]
        dst_wo = wo[:, 0:8 * 128].rearrange("p (c n) -> p c n", c=8)
        p.load("wo", lambda e, a=dst_wo, s_=src_wo: e.dma_start(out=a, in_=s_), queue=wqueue())
        for tt in range(nqb):
            pb = ctx.bank()

            def womm(eng, nc_=nc_, tt=tt, pb=pb, wo=wo):
                last = None
                for mc in range(8):
                    last = eng.matmul(
                        pb[:, :],
                        wo[:, mc * 128:(mc + 1) * 128],
                        opair[:, mc * T + tt * 512: mc * T + tt * 512 + 512],
                        start=(mc == 0), stop=(mc == 7))
                return last

            p.comp("tensor", womm, needs=("wo",))
            p.comp("vector", lambda e, nc_=nc_, tt=tt, pb=pb: e.tensor_add(
                out=xT_sb[:, nc_ * T + tt * 512: nc_ * T + tt * 512 + 512],
                in0=xT_sb[:, nc_ * T + tt * 512: nc_ * T + tt * 512 + 512],
                in1=pb[:, :]))
        p.mark("wo")



def _common_consts(nc, ctx, T, with_tstage=True):
    dram = {}
    for name, shape, dt in [
        ("ident", [128, 128], F32),
        ("ones_col", [128, 1], BF16),
        ("ones_row", [1, 128], F32),
    ]:
        dram[name] = nc.dram_tensor("d_" + name, shape, dt, kind="ExternalInput").ap()
    dram["cpool"] = nc.dram_tensor("d_cpool", [128, 32], F32, kind="ExternalInput").ap()
    ctx.sbuf("ident", [128, 128], F32)
    ctx.sbuf("ones_col", [128, 1], BF16)
    ctx.sbuf("ones_row", [1, 128], F32)
    ctx.sbuf("cpool", [128, 32], F32)
    p = ctx.p
    for n in ("ident", "ones_col", "ones_row", "cpool"):
        p.load("consts", lambda e, n=n: e.dma_start(out=ctx.bufs[n][:], in_=dram[n][:]),
               queue="sync")
    for n in ("st_sum", "st_ssq", "st_m", "st_e2", "st_tmp", "st_rstd"):
        ctx.sbuf(n, [1, T], F32)
    ctx.sbuf("mb", [128, T], F32)
    ctx.sbuf("rb", [128, T], F32)
    ctx.sbuf("sqscr", [128, 2 * 512], BF16)
    ctx.sbuf("tscr", [128, 2 * T], F32)
    if with_tstage:
        ctx.sbuf("tstage", [128, 2048], F32)
    return dram


def const_inputs(T):
    import ml_dtypes
    out = {
        "ident": np.eye(128, dtype=np.float32),
        "ones_col": np.ones((128, 1), dtype=ml_dtypes.bfloat16),
        "ones_row": np.ones((1, 128), dtype=np.float32),
    }
    return out


def build_l0attn(T=T_SEQ, nseq=2, with_in_transpose=True):
    nc = bass.Bass("TRN2", target_bir_lowering=False)
    es = ExitStack()
    p = (Prog2 if USE_PROG2 else Prog)(nc, es)
    ctx = Ctx(nc, es, p)
    dram = _common_consts(nc, ctx, 512)
    for s in range(nseq):
        dram[f"x_s{s}"] = nc.dram_tensor(f"x_s{s}", [T, D_MODEL], F32, kind="ExternalInput").ap()
        dram[f"xfm_s{s}"] = nc.dram_tensor(f"xfm_s{s}", [D_MODEL, T], F32, kind="ExternalOutput").ap()
    for w in ("WqT", "WkT", "WvT", "WoT"):
        dram[w] = nc.dram_tensor(w, [D_MODEL, D_MODEL], F32, kind="ExternalInput").ap()
    dram["cmask"] = nc.dram_tensor("d_cmask", [128, 4 * 512], BF16, kind="ExternalInput").ap()
    ctx.dram = dram

    xT = ctx.sbuf("xT", [128, 8 * T], F32)
    lnb = ctx.sbuf("lnb", [128, 8 * T], BF16)
    ctx.sbuf("qT", [128, T], BF16)
    ctx.sbuf("kT", [128, T], BF16)
    ctx.sbuf("v0", [128, (T // 128) * 65], BF16)
    ctx.sbuf("v1", [128, (T // 128) * 65], BF16)
    ctx.sbuf("opair", [128, 8 * T], BF16)
    ctx.sbuf("expscr", [128, 4 * 512], BF16)
    ctx.sbuf("rlscr", [128, 2 * 512], F32)
    ctx.sbuf("wq3", [128, 2 * 3 * 1024], BF16)
    ctx.sbuf("wo", [128, 2 * 8 * 128], BF16)
    ctx.sbuf("cmask", [128, 4 * 512], BF16)
    p.load("consts", lambda e: e.dma_start(out=ctx.bufs["cmask"][:], in_=dram["cmask"][:]),
           queue="sync")

    v0_b = ctx.bufs["v0"]
    v1_b = ctx.bufs["v1"]

    def ones_cols(eng, v0_b=v0_b, v1_b=v1_b):
        last = None
        for tt in range(T // 128):
            last = eng.memset(v0_b[:, tt * 65 + 64: tt * 65 + 65], 1.0)
        for tt in range(T // 128):
            last = eng.memset(v1_b[:, tt * 65 + 64: tt * 65 + 65], 1.0)
        return last

    p.comp("vector", ones_cols, needs=("consts",))

    for s in range(nseq):
        p.gate_stores()
        emit_transpose_in(ctx, dram[f"x_s{s}"], xT, T)
        for rg in range(T // 512):
            emit_ln_fm(ctx, xT, 512, lnb, stride=T, t0=rg * 512,
                       lnb_stride=T, lnb_t0=rg * 512)
        emit_l0_attention(ctx, T, lnb, xT)
        for c in range(8):
            p.store(lambda e, c=c, s=s: e.dma_start(
                out=dram[f"xfm_s{s}"][c * 128:(c + 1) * 128, :],
                in_=xT[:, c * T:(c + 1) * T]))
    p.finish()
    p.emit()
    nc.finalize()
    return nc, es, {"const_vals": dict(ctx.const_vals)}


def cpool_array(const_vals):
    cp = np.zeros((128, 32), dtype=np.float32)
    for v, i in const_vals.items():
        cp[:, i] = v
    return cp


def l0attn_const_inputs(T=T_SEQ):
    import ml_dtypes
    out = const_inputs(T)
    i = np.arange(128)[:, None]
    j = np.arange(512)[None, :]
    m = np.concatenate([(128 * r + i <= j) for r in range(4)], axis=1)
    out["cmask"] = m.astype(ml_dtypes.bfloat16)
    return out


def emit_l0_moe(ctx, T, lnb_sb, xT_sb, debug_stage=99):
    p, nc = ctx.p, ctx.nc
    dram = ctx.dram
    at_sb = ctx.bufs["at_sb"]
    bwt_sb = ctx.bufs["bwt_sb"]
    wrt_sb = ctx.bufs["wrt_sb"]
    midall = ctx.bufs["midall"]
    mb = ctx.bufs["mb"]; rb = ctx.bufs["rb"]; tscr = ctx.bufs["tscr"]
    grow = ctx.bufs["grow"]
    gcol = ctx.bufs["gcol"]
    ones_row = ctx.bufs["ones_row"]
    ntt = T // 512

    pg = ctx.bank()

    def gmm(eng, pg=pg):
        last = None
        for m in range(8):
            last = eng.matmul(pg[0:1, 0:8], lnb_sb[:, m * T: m * T + 1],
                              wrt_sb[:, m * 8:(m + 1) * 8],
                              start=(m == 0), stop=(m == 7))
        return last

    p.comp("tensor", gmm, needs=("moew",))
    p.comp("vector", lambda e, pg=pg: e.tensor_copy(out=grow[:, 0:8], in_=pg[0:1, 0:8]))
    p.comp("vector", lambda e: e.reduce_max(out=grow[:, 8:9], in_=grow[:, 0:8], axis=AX.X))
    p.comp("vector", lambda e: e.tensor_scalar(out=grow[:, 9:10], in0=grow[:, 8:9],
                                               scalar1=-1.0, scalar2=None, op0=ALU.mult))
    p.comp("vector", lambda e: e.tensor_scalar(out=grow[:, 16:24], in0=grow[:, 0:8],
                                               scalar1=grow[:, 8:9], scalar2=None, op0=ALU.is_ge))
    p.comp("vector", lambda e: e.scalar_tensor_tensor(
        out=grow[:, 16:24], in0=grow[:, 16:24], scalar=-1e30, in1=grow[:, 0:8],
        op0=ALU.mult, op1=ALU.add))
    p.comp("vector", lambda e: e.reduce_max(out=grow[:, 10:11], in_=grow[:, 16:24], axis=AX.X))
    p.comp("scalar", lambda e: e.activation(out=grow[:, 24:32], in_=grow[:, 0:8],
                                            func=AF.Exp, bias=grow[:, 9:10]))
    p.comp("vector", lambda e: e.tensor_scalar(out=grow[:, 16:24], in0=grow[:, 0:8],
                                               scalar1=grow[:, 10:11], scalar2=None, op0=ALU.is_ge))
    p.comp("vector", lambda e: e.tensor_mul(out=grow[:, 24:32], in0=grow[:, 24:32], in1=grow[:, 16:24]))
    p.comp("vector", lambda e: e.reduce_sum(out=grow[:, 11:12], in_=grow[:, 24:32], axis=AX.X))
    p.comp("vector", lambda e: e.reciprocal(out=grow[:, 12:13], in_=grow[:, 11:12]))
    p.comp("vector", lambda e: e.tensor_scalar(out=grow[:, 24:32], in0=grow[:, 24:32],
                                               scalar1=grow[:, 12:13], scalar2=None, op0=ALU.mult))
    pgc = ctx.bank()

    def gbc(eng, pgc=pgc):
        return eng.matmul(pgc[0:64, 0:8], ones_row[0:1, 0:64], grow[:, 24:32],
                          start=True, stop=True)

    p.comp("tensor", gbc)
    p.comp("vector", lambda e, pgc=pgc: e.tensor_copy(out=gcol[:, 0:8], in_=pgc[0:64, 0:8]))

    if debug_stage < 3:
        return
    for e_ in range(8):
        for tt in range(ntt):
            pm = ctx.bank()

            def amm(eng, e_=e_, tt=tt, pm=pm):
                last = None
                for m in range(8):
                    last = eng.matmul(
                        pm[0:64, :],
                        at_sb[:, m * 512 + e_ * 64: m * 512 + (e_ + 1) * 64],
                        lnb_sb[:, m * T + tt * 512: m * T + tt * 512 + 512],
                        start=(m == 0), stop=(m == 7))
                return last

            p.comp("tensor", amm, needs=("moew",))
            ro = (e_ % 2) * 64
            p.comp("vector", lambda e, e_=e_, tt=tt, pm=pm, ro=ro: e.tensor_scalar(
                out=midall[ro:ro + 64, (e_ // 2) * T + tt * 512:(e_ // 2) * T + tt * 512 + 512],
                in0=pm[0:64, :], scalar1=gcol[:, e_:e_ + 1], scalar2=None, op0=ALU.mult))

    if debug_stage < 4:
        return
    ones_row_f = ctx.bufs["ones_row"]
    smf = ctx.bufs["st_m_full"]
    srf = ctx.bufs["st_r_full"]
    for tt in range(ntt):
        pm, pr2 = ctx.bank(), ctx.bank()
        p.comp("tensor", lambda e, tt=tt, pm=pm, smf=smf: e.matmul(
            pm[:, :], ones_row_f[0:1, :],
            smf[:, tt * 512:(tt + 1) * 512], start=True, stop=True))
        p.comp("vector", lambda e, pm=pm: e.tensor_copy(out=mb[:, 0:512], in_=pm[:, :]))
        p.comp("tensor", lambda e, tt=tt, pr2=pr2, srf=srf: e.matmul(
            pr2[:, :], ones_row_f[0:1, :],
            srf[:, tt * 512:(tt + 1) * 512], start=True, stop=True))
        p.comp("vector", lambda e, pr2=pr2: e.tensor_copy(out=rb[:, 0:512], in_=pr2[:, :]))
        if debug_stage < 5:
            continue
        for hc in range(8):
            src_bw = ctx.wsrc["BwTp"].rearrange("p (j n) -> p j n", j=4)[:, :, hc * 128:(hc + 1) * 128] if hasattr(ctx, "wsrc") else dram["BwTp"].rearrange("p (j n) -> p j n", j=4)[:, :, hc * 128:(hc + 1) * 128]
            dst_bw = bwt_sb[:, 0:4 * 128].rearrange("p (j n) -> p j n", j=4)
            p.load("bwt", lambda e, a=dst_bw, s_=src_bw: e.dma_start(out=a, in_=s_),
                   queue=(ctx.wqueue() if hasattr(ctx, "wqueue") else "gpsimd"))
            pd = ctx.bank()

            def bmm(eng, hc=hc, tt=tt, pd=pd):
                last = None
                for j in range(4):
                    last = eng.matmul(
                        pd[:, :],
                        bwt_sb[:, j * 128:(j + 1) * 128],
                        midall[:, j * T + tt * 512: j * T + tt * 512 + 512],
                        start=(j == 0), stop=(j == 3))
                return last

            if debug_stage >= 46:
                p.comp("tensor", bmm, needs=("bwt",))
            else:
                p.comp("vector", lambda e: e.engine_nop(), needs=("bwt",))
            p.mark("bwt")
            xs = xT_sb[:, hc * T + tt * 512: hc * T + tt * 512 + 512]
            if debug_stage < 6:
                if debug_stage >= 46:
                    p.comp("vector", lambda e, pd=pd: e.tensor_copy(out=tscr[:, 0:512], in_=pd[:, :]))
                continue
            p.comp("vector", lambda e, xs=xs: e.tensor_sub(
                out=tscr[:, 0:512], in0=xs, in1=mb[:, 0:512]))
            p.comp("vector", lambda e: e.tensor_mul(
                out=tscr[:, 0:512], in0=tscr[:, 0:512], in1=rb[:, 0:512]))
            p.comp("vector", lambda e, xs=xs: e.tensor_add(out=xs, in0=xs, in1=tscr[:, 0:512]))
            p.comp("vector", lambda e, xs=xs, pd=pd: e.tensor_add(out=xs, in0=xs, in1=pd[:, :]))


def build_l0ffn(T=T_SEQ, nseq=2, debug_stage=99):
    nc = bass.Bass("TRN2", target_bir_lowering=False)
    es = ExitStack()
    p = (Prog2 if USE_PROG2 else Prog)(nc, es)
    ctx = Ctx(nc, es, p)
    dram = _common_consts(nc, ctx, 512, with_tstage=False)
    for s in range(nseq):
        dram[f"xfm_in_s{s}"] = nc.dram_tensor(f"xfm_in_s{s}", [D_MODEL, T], F32, kind="ExternalInput").ap()
        dram[f"xfm_s{s}"] = nc.dram_tensor(f"xfm_s{s}", [D_MODEL, T], F32, kind="ExternalOutput").ap()
    dram["WrT"] = nc.dram_tensor("WrT", [D_MODEL, 8], F32, kind="ExternalInput").ap()
    dram["AT"] = nc.dram_tensor("AT", [D_MODEL, 512], F32, kind="ExternalInput").ap()
    dram["BwTp"] = nc.dram_tensor("BwTp", [128, 4096], F32, kind="ExternalInput").ap()
    ctx.dram = dram

    xT = ctx.sbuf("xT", [128, 8 * T], F32)
    lnb = ctx.sbuf("lnb", [128, 8 * T], BF16)
    ctx.sbuf("at_sb", [128, 8 * 512], BF16)
    ctx.sbuf("bwt_sb", [128, 4 * 128], BF16)
    ctx.sbuf("wrt_sb", [128, 8 * 8], BF16)
    ctx.sbuf("midall", [128, 4 * T], BF16)
    ctx.sbuf("grow", [1, 32], F32)
    ctx.sbuf("gcol", [64, 8], F32)
    ctx.sbuf("st_m_full", [1, T], F32)
    ctx.sbuf("st_r_full", [1, T], F32)

    for m in range(8):
        p.load("moew", lambda e, m=m: e.dma_start(
            out=ctx.bufs["at_sb"][:, m * 512:(m + 1) * 512],
            in_=dram["AT"][m * 128:(m + 1) * 128, :]), queue="gpsimd")
        p.load("moew", lambda e, m=m: e.dma_start(
            out=ctx.bufs["wrt_sb"][:, m * 8:(m + 1) * 8],
            in_=dram["WrT"][m * 128:(m + 1) * 128, :]), queue="gpsimd")

    for s in range(nseq):
        for c in range(8):
            p.load(f"xin{c % 2}", lambda e, c=c, s=s: e.dma_start(
                out=xT[:, c * T:(c + 1) * T],
                in_=dram[f"xfm_in_s{s}"][c * 128:(c + 1) * 128, :]), queue="sync",
                gate_stores=True)
        p.comp("vector", lambda e: e.engine_nop(), needs=("xin0", "xin1", "consts"))
        if debug_stage >= 1:
            for rg in range(T // 512):
                emit_ln_fm(ctx, xT, 512, lnb, stride=T, t0=rg * 512,
                           lnb_stride=T, lnb_t0=rg * 512,
                           m_out=ctx.bufs["st_m_full"][:, rg * 512:(rg + 1) * 512],
                           r_out=ctx.bufs["st_r_full"][:, rg * 512:(rg + 1) * 512])
        if debug_stage >= 2:
            emit_l0_moe(ctx, T, lnb, xT, debug_stage=debug_stage)
        p.mark("xin0")
        p.mark("xin1")
        for c in range(8):
            p.store(lambda e, c=c, s=s: e.dma_start(
                out=dram[f"xfm_s{s}"][c * 128:(c + 1) * 128, :],
                in_=xT[:, c * T:(c + 1) * T]))
    p.finish()
    p.emit()
    nc.finalize()
    return nc, es, {"const_vals": dict(ctx.const_vals)}


def l0ffn_weight_inputs(Wr, A, Bw):
    AT = np.ascontiguousarray(np.transpose(np.asarray(A), (2, 0, 1)).reshape(1024, 512))
    bw = np.zeros((128, 4096), dtype=np.float32)
    Bw = np.asarray(Bw)
    for j in range(4):
        bw[0:64, j * 1024:(j + 1) * 1024] = Bw[2 * j].T
        bw[64:128, j * 1024:(j + 1) * 1024] = Bw[2 * j + 1].T
    return {"WrT": np.ascontiguousarray(np.asarray(Wr).T), "AT": AT, "BwTp": bw}


def emit_l1_retention(ctx, T, xT_sb):
    p, nc = ctx.p, ctx.nc
    dram = ctx.dram
    ws = getattr(ctx, "wsrc", dram)
    wqueue = getattr(ctx, "wqueue", lambda: "gpsimd")
    R = 512
    nrange = T // R
    NH = 4
    b = ctx.bufs
    qk_sb, wv_sb, wol1 = b["qk_sb"], b["wv_sb"], b["wol1"]
    lnr = b["lnr"]
    qraw, qhat, khat = b["qraw"], b["qhat"], b["khat"]
    vtile, obuf = b["vtile"], b["obuf"]
    gb, ofscr, ofT = b["gb"], b["ofscr"], b["ofT"]
    qts, kts, ktok = b["qts"], b["kts"], b["ktok"]
    sqs, racc = b["sqs2"], b["racc"]
    Sf, Sb = b["Sf"], b["Sb"]
    ropes = b["ropes"]
    dt_t, cqb, ckb, gnb = b["dt_t"], b["cqb"], b["ckb"], b["gnb"]
    ident_bf = b["ident_bf"]
    eps_ap = ctx.constap(1e-5, 128)
    GAMMA = [1.0 - 2.0 ** (-5.0 - h) for h in range(NH)]

    p.comp("vector", lambda e: e.memset(Sf[:], 0.0))
    p.comp("vector", lambda e: e.memset(Sb[:], 0.0))

    for rg in range(nrange):
        t0 = rg * R
        emit_ln_fm(ctx, xT_sb, R, lnr, stride=T, t0=t0, lnb_stride=R, lnb_t0=0)
        for j, nm in enumerate(("cosq", "sinq", "cosk", "sink")):
            p.load("ropes", lambda e, j=j, nm=nm, t0=t0: e.dma_start(
                out=ropes[:, j * 512:(j + 1) * 512], in_=dram[nm][:, t0:t0 + R]),
                queue="sync")
        for h in range(NH):
            wslot = "l1qk"
            src_q = ws["Wq"].rearrange("(c p) n -> p c n", p=128)[:, :, h * 256:(h + 1) * 256]
            src_k = ws["Wk"].rearrange("(c p) n -> p c n", p=128)[:, :, h * 256:(h + 1) * 256]
            dst_q = qk_sb[:, 0:8 * 256].rearrange("p (c n) -> p c n", c=8)
            dst_k = qk_sb[:, 8 * 256:16 * 256].rearrange("p (c n) -> p c n", c=8)
            p.load(wslot, lambda e, a=dst_q, s_=src_q: e.dma_start(out=a, in_=s_), queue=wqueue())
            p.load(wslot, lambda e, a=dst_k, s_=src_k: e.dma_start(out=a, in_=s_), queue=wqueue())
            src_v = ws["Wv"].rearrange("(c p) n -> p c n", p=128)[:, :, h * 512:(h + 1) * 512]
            dst_v = wv_sb[:, 0:8 * 512].rearrange("p (c n) -> p c n", c=8)
            p.load("l1v", lambda e, a=dst_v, s_=src_v: e.dma_start(out=a, in_=s_), queue=wqueue())

            for (dst, base, tabofs) in ((qhat, 0, 0), (khat, 2048, 1024)):
                for cd in range(2):
                    pq = ctx.bank()

                    def qmm(eng, base=base, cd=cd, pq=pq):
                        last = None
                        for mc in range(8):
                            last = eng.matmul(
                                pq[:, :],
                                qk_sb[:, base + mc * 256 + cd * 128: base + mc * 256 + cd * 128 + 128],
                                lnr[:, mc * R: (mc + 1) * R],
                                start=(mc == 0), stop=(mc == 7))
                        return last

                    p.comp("tensor", qmm, needs=(wslot,))
                    p.comp("vector", lambda e, cd=cd, pq=pq: e.tensor_copy(
                        out=qraw[:, cd * 512:(cd + 1) * 512], in_=pq[:, :]))
                cc = ropes[:, tabofs + 0:tabofs + 512]
                ss = ropes[:, tabofs + 512:tabofs + 1024] if tabofs == 0 else ropes[:, 1536:2048]
                cc = ropes[:, (0 if tabofs == 0 else 1024):(0 if tabofs == 0 else 1024) + 512]
                ss = ropes[:, (512 if tabofs == 0 else 1536):(512 if tabofs == 0 else 1536) + 512]
                A = qraw[:, 0:512]
                B = qraw[:, 512:1024]
                Ao = dst[:, 0:512]
                Bo = dst[:, 512:1024]
                p.comp("vector", lambda e, A=A, cc=cc, Ao=Ao: e.tensor_mul(out=Ao, in0=A, in1=cc), needs=("ropes",))
                p.comp("vector", lambda e, B=B, ss=ss: e.tensor_mul(out=gb[:], in0=B, in1=ss))
                p.comp("vector", lambda e, Ao=Ao: e.tensor_sub(out=Ao, in0=Ao, in1=gb[:]))
                p.comp("vector", lambda e, B=B, cc=cc, Bo=Bo: e.tensor_mul(out=Bo, in0=B, in1=cc))
                p.comp("vector", lambda e, A=A, ss=ss: e.tensor_mul(out=gb[:], in0=A, in1=ss))
                p.comp("vector", lambda e, Bo=Bo: e.tensor_add(out=Bo, in0=Bo, in1=gb[:]))

            for n in range(4):
                pv = ctx.bank()

                def vmm(eng, n=n, pv=pv):
                    last = None
                    for mc in range(8):
                        last = eng.matmul(
                            pv[:, :],
                            lnr[:, mc * R + n * 128: mc * R + (n + 1) * 128],
                            wv_sb[:, mc * 512:(mc + 1) * 512],
                            start=(mc == 0), stop=(mc == 7))
                    return last

                p.comp("tensor", vmm, needs=("l1v",))
                p.comp("vector", lambda e, n=n, pv=pv: e.tensor_copy(
                    out=vtile[:, n * 512:(n + 1) * 512], in_=pv[:, :]))
            p.mark("l1v")
            src_g = ws["Wg"].rearrange("(c p) n -> p c n", p=128)[:, :, h * 512:(h + 1) * 512]
            dst_g = wv_sb[:, 0:8 * 512].rearrange("p (c n) -> p c n", c=8)
            p.load("l1v", lambda e, a=dst_g, s_=src_g: e.dma_start(out=a, in_=s_), queue=wqueue())

            gC = GAMMA[h] ** 128
            for n in range(4):
                pa = ctx.bank()

                def amm(eng, pa=pa, n=n):
                    last = None
                    for cd in range(2):
                        last = eng.matmul(
                            pa[:, 0:128],
                            khat[:, cd * 512 + n * 128: cd * 512 + (n + 1) * 128],
                            qhat[:, cd * 512 + n * 128: cd * 512 + (n + 1) * 128],
                            start=(cd == 0), stop=(cd == 1))
                    return last

                p.comp("tensor", amm)
                p.comp("vector", lambda e, pa=pa, h=h: e.tensor_mul(
                    out=sqs[:, 0:128], in0=pa[:, 0:128], in1=dt_t[:, h * 128:(h + 1) * 128]))
                for cd in range(2):
                    p.comp("vector", lambda e, cd=cd, n=n, h=h: e.tensor_mul(
                        out=qts[:, cd * 128:(cd + 1) * 128],
                        in0=qhat[:, cd * 512 + n * 128: cd * 512 + (n + 1) * 128],
                        in1=cqb[:, h * 128:(h + 1) * 128]))
                po = ctx.reserve()

                def omm(eng, po=po, n=n, h=h):
                    eng.matmul(po[:, :], sqs[:, 0:128], vtile[:, n * 512:(n + 1) * 512],
                               start=True, stop=False)
                    last = None
                    for cd in range(2):
                        last = eng.matmul(
                            po[:, :], qts[:, cd * 128:(cd + 1) * 128],
                            Sb[:, (h * 2 + cd) * 512:(h * 2 + cd + 1) * 512],
                            start=False, stop=(cd == 1))
                    return last

                p.comp("tensor", omm)
                ctx.release(po)
                p.comp("vector", lambda e, po=po, n=n: e.tensor_copy(
                    out=obuf[:, n * 512:(n + 1) * 512], in_=po[:, :]))
                for cd in range(2):
                    p.comp("vector", lambda e, cd=cd, n=n, h=h: e.tensor_mul(
                        out=kts[:, cd * 128:(cd + 1) * 128],
                        in0=khat[:, cd * 512 + n * 128: cd * 512 + (n + 1) * 128],
                        in1=ckb[:, h * 128:(h + 1) * 128]))

                def ktr(eng):
                    last = None
                    for cd in range(2):
                        last = eng.matmul(ctx.psbf[:, cd * 128:(cd + 1) * 128],
                                          kts[:, cd * 128:(cd + 1) * 128], ident_bf[:],
                                          start=(cd == 0), stop=(cd == 1), is_transpose=True)
                    return last

                p.comp("tensor", ktr)
                p.comp("vector", lambda e: e.tensor_copy(out=ktok[:, 0:256], in_=ctx.psbf[:, 0:256]))
                for cd in range(2):
                    pd = ctx.bank()
                    p.comp("tensor", lambda e, cd=cd, n=n, pd=pd: e.matmul(
                        pd[:, :], ktok[:, cd * 128:(cd + 1) * 128],
                        vtile[:, n * 512:(n + 1) * 512], start=True, stop=True))
                    sfs = Sf[:, (h * 2 + cd) * 512:(h * 2 + cd + 1) * 512]
                    sbs = Sb[:, (h * 2 + cd) * 512:(h * 2 + cd + 1) * 512]
                    p.comp("vector", lambda e, sfs=sfs, pd=pd, gC=gC: e.scalar_tensor_tensor(
                        out=sfs, in0=sfs, scalar=gC, in1=pd[:, :], op0=ALU.mult, op1=ALU.add))
                    p.comp("vector", lambda e, sfs=sfs, sbs=sbs: e.tensor_copy(out=sbs, in_=sfs))

            for n in range(4):
                pg = ctx.bank()

                def gmm(eng, n=n, pg=pg):
                    last = None
                    for mc in range(8):
                        last = eng.matmul(
                            pg[:, :],
                            lnr[:, mc * R + n * 128: mc * R + (n + 1) * 128],
                            wv_sb[:, mc * 512:(mc + 1) * 512],
                            start=(mc == 0), stop=(mc == 7))
                    return last

                p.comp("tensor", gmm, needs=("l1v",))
                p.comp("scalar", lambda e, pg=pg: e.activation(out=gb[:], in_=pg[:, :], func=AF.Silu))
                p.comp("scalar", lambda e, n=n: e.activation(
                    out=sqs[:], in_=obuf[:, n * 512:(n + 1) * 512], func=AF.Square,
                    accum_out=racc[:, 0:1]))
                p.comp("scalar", lambda e, eps_ap=eps_ap: e.activation(
                    out=racc[:, 1:2], in_=racc[:, 0:1], func=AF.Ln, scale=1.0 / 512.0, bias=eps_ap))
                p.comp("scalar", lambda e: e.activation(
                    out=racc[:, 1:2], in_=racc[:, 1:2], func=AF.Exp, scale=-0.5))
                p.comp("vector", lambda e, n=n: e.tensor_mul(
                    out=ofscr[:], in0=obuf[:, n * 512:(n + 1) * 512], in1=gb[:]))
                p.comp("vector", lambda e: e.tensor_scalar(
                    out=ofscr[:], in0=ofscr[:], scalar1=racc[:, 1:2], scalar2=None, op0=ALU.mult))
                p.comp("vector", lambda e: e.tensor_mul(out=ofscr[:], in0=ofscr[:], in1=gnb[:]))

                def otr(eng, n=n):
                    last = None
                    for c in range(4):
                        last = eng.matmul(ctx.psbf[:, 512 + c * 128: 512 + (c + 1) * 128],
                                          ofscr[:, c * 128:(c + 1) * 128], ident_bf[:],
                                          start=(c == 0), stop=(c == 3), is_transpose=True)
                    return last

                p.comp("tensor", otr)

                def oev(eng, n=n, h=h):
                    last = None
                    for c in range(4):
                        last = eng.tensor_copy(
                            out=ofT[:, (4 * h + c) * 512 + n * 128: (4 * h + c) * 512 + (n + 1) * 128],
                            in_=ctx.psbf[:, 512 + c * 128: 512 + (c + 1) * 128])
                    return last

                p.comp("vector", oev)
            p.mark(wslot)
            p.mark("l1v")

        for dc in range(8):
            src_wo = ws["Wo"].rearrange("(c p) n -> p c n", p=128)[:, :, dc * 128:(dc + 1) * 128]
            dst_wo = wol1[:, 0:16 * 128].rearrange("p (c n) -> p c n", c=16)
            p.load("wol1", lambda e, a=dst_wo, s_=src_wo: e.dma_start(out=a, in_=s_), queue=wqueue())
            pw = ctx.bank()

            def wmm(eng, pw=pw, dc=dc):
                last = None
                for ic in range(16):
                    last = eng.matmul(pw[:, :], wol1[:, ic * 128:(ic + 1) * 128],
                                      ofT[:, ic * 512:(ic + 1) * 512],
                                      start=(ic == 0), stop=(ic == 15))
                return last

            p.comp("tensor", wmm, needs=("wol1",))
            p.comp("vector", lambda e, dc=dc, pw=pw, t0=t0: e.tensor_add(
                out=xT_sb[:, dc * T + t0: dc * T + t0 + R],
                in0=xT_sb[:, dc * T + t0: dc * T + t0 + R], in1=pw[:, :]))
            p.mark("wol1")
        p.mark("ropes")


def build_l1attn(T=T_SEQ, nseq=2):
    nc = bass.Bass("TRN2", target_bir_lowering=False)
    es = ExitStack()
    p = (Prog2 if USE_PROG2 else Prog)(nc, es)
    ctx = Ctx(nc, es, p)
    dram = _common_consts(nc, ctx, 512)
    for s in range(nseq):
        dram[f"xfm_in_s{s}"] = nc.dram_tensor(f"xfm_in_s{s}", [D_MODEL, T], F32, kind="ExternalInput").ap()
        dram[f"xfm_s{s}"] = nc.dram_tensor(f"xfm_s{s}", [D_MODEL, T], F32, kind="ExternalOutput").ap()
    dram["Wq"] = nc.dram_tensor("Wq", [1024, 1024], F32, kind="ExternalInput").ap()
    dram["Wk"] = nc.dram_tensor("Wk", [1024, 1024], F32, kind="ExternalInput").ap()
    dram["Wv"] = nc.dram_tensor("Wv", [1024, 2048], F32, kind="ExternalInput").ap()
    dram["Wg"] = nc.dram_tensor("Wg", [1024, 2048], F32, kind="ExternalInput").ap()
    dram["Wo"] = nc.dram_tensor("Wo", [2048, 1024], F32, kind="ExternalInput").ap()
    for nm in ("cosq", "sinq", "cosk", "sink"):
        dram[nm] = nc.dram_tensor(nm, [128, T], BF16, kind="ExternalInput").ap()
    for nm, w in (("dt_t", 4 * 128), ("cqb", 4 * 128), ("ckb", 4 * 128),
                  ("gnb", 512), ("ident_bf", 128)):
        dram[nm] = nc.dram_tensor("d_" + nm, [128, w], BF16, kind="ExternalInput").ap()
        ctx.sbuf(nm, [128, w], BF16)
        p.load("consts", lambda e, nm=nm: e.dma_start(out=ctx.bufs[nm][:], in_=dram[nm][:]),
               queue="sync")
    ctx.dram = dram

    xT = ctx.sbuf("xT", [128, 8 * T], F32)
    ctx.sbuf("lnr", [128, 8 * 512], BF16)
    ctx.sbuf("qk_sb", [128, 16 * 256], BF16)
    ctx.sbuf("wv_sb", [128, 8 * 512], BF16)
    ctx.sbuf("wol1", [128, 16 * 128], BF16)
    ctx.sbuf("qraw", [128, 2 * 512], BF16)
    ctx.sbuf("qhat", [128, 2 * 512], BF16)
    ctx.sbuf("khat", [128, 2 * 512], BF16)
    ctx.sbuf("vtile", [128, 4 * 512], BF16)
    ctx.sbuf("obuf", [128, 4 * 512], BF16)
    ctx.sbuf("gb", [128, 512], BF16)
    ctx.sbuf("ofscr", [128, 512], BF16)
    ctx.sbuf("ofT", [128, 16 * 512], BF16)
    ctx.sbuf("qts", [128, 2 * 128], BF16)
    ctx.sbuf("kts", [128, 2 * 128], BF16)
    ctx.sbuf("ktok", [128, 256], BF16)
    ctx.sbuf("sqs2", [128, 512], BF16)
    ctx.sbuf("racc", [128, 2], F32)
    ctx.sbuf("Sf", [128, 8 * 512], F32)
    ctx.sbuf("Sb", [128, 8 * 512], BF16)
    ctx.sbuf("ropes", [128, 4 * 512], BF16)

    for s in range(nseq):
        for c in range(8):
            p.load(f"xin{c % 2}", lambda e, c=c, s=s: e.dma_start(
                out=xT[:, c * T:(c + 1) * T],
                in_=dram[f"xfm_in_s{s}"][c * 128:(c + 1) * 128, :]), queue="sync",
                gate_stores=True)
        p.comp("vector", lambda e: e.engine_nop(), needs=("xin0", "xin1", "consts"))
        emit_l1_retention(ctx, T, xT)
        p.mark("xin0")
        p.mark("xin1")
        for c in range(8):
            p.store(lambda e, c=c, s=s: e.dma_start(
                out=dram[f"xfm_s{s}"][c * 128:(c + 1) * 128, :],
                in_=xT[:, c * T:(c + 1) * T]))
    p.finish()
    p.emit()
    nc.finalize()
    return nc, es, {"const_vals": dict(ctx.const_vals)}


def l1attn_const_inputs(T=T_SEQ):
    import ml_dtypes
    out = {}
    d = 256
    inv_freq = 1.0 / (10000.0 ** (np.arange(0, d, 2, dtype=np.float64) / d))
    t = np.arange(T, dtype=np.float64)
    fr = np.outer(t, inv_freq)
    cos = np.cos(fr).T
    sin = np.sin(fr).T
    out["cosq"] = (cos / 16.0).astype(ml_dtypes.bfloat16)
    out["sinq"] = (sin / 16.0).astype(ml_dtypes.bfloat16)
    out["cosk"] = cos.astype(ml_dtypes.bfloat16)
    out["sink"] = sin.astype(ml_dtypes.bfloat16)
    NH = 4
    dt_t = np.zeros((128, NH * 128), np.float64)
    cqb = np.zeros((128, NH * 128), np.float64)
    ckb = np.zeros((128, NH * 128), np.float64)
    i = np.arange(128)
    for h in range(NH):
        g = 1.0 - 2.0 ** (-5.0 - h)
        D = np.where(i[:, None] >= i[None, :], g ** (i[:, None] - i[None, :]), 0.0)
        dt_t[:, h * 128:(h + 1) * 128] = D.T
        cqb[:, h * 128:(h + 1) * 128] = np.broadcast_to(g ** (i + 1), (128, 128))
        ckb[:, h * 128:(h + 1) * 128] = np.broadcast_to(g ** (127 - i), (128, 128))
    out["dt_t"] = dt_t.astype(ml_dtypes.bfloat16)
    out["cqb"] = cqb.astype(ml_dtypes.bfloat16)
    out["ckb"] = ckb.astype(ml_dtypes.bfloat16)
    out["ident_bf"] = np.eye(128).astype(ml_dtypes.bfloat16)
    return out


A_CHUNKS = [(i * 128, 128) for i in range(21)] + [(2688, 42)]
OC_BLOCKS = [A_CHUNKS[i:i + 4] for i in range(0, 20, 4)] + [A_CHUNKS[20:22]]


def emit_l1ffn_layer(ctx, T, xI, li):
    p, nc = ctx.p, ctx.nc
    dram = ctx.dram
    b = ctx.bufs
    R = 512
    nrange = T // R
    TP = T + 1
    wbig = b["wbig"]
    wbig2 = b.get("wbig2", wbig)
    qscf = b["qsc"]
    tsc2f = b["tsc2"]
    bnd = b["bnd"]
    xsr = b["xoutr"]
    rperm = b["rperm"]
    hsel = b["hsel"]
    qsind = b["qsind"]
    lngc, lnbc = b["lngc"], b["lnbc"]
    b1ac, b1gc, b2c = b["b1ac"], b["b1gc"], b["b2c"]
    lmask = b["lmask"]
    rnorm2f = b["rnorm2"]

    def x_ap(c, t0, L):
        ofs = c * TP + 1 + t0
        return xI[:, ofs: ofs + L]

    def xs_ap(c, t0, L):
        ofs = c * TP + (0 if c >= 4 else 1) + t0
        return xI[:, ofs: ofs + L]

    def build_xsr(t0, rg, which):
        par = rg % 2
        ofs = which * 16 + par * 8
        for c in range(8):
            if c < 4:
                p.comp("vector", lambda e, c=c, t0=t0: e.tensor_copy(
                    out=xsr[:, c * 512:(c + 1) * 512], in_=x_ap(c, t0, R)))
            else:
                p.comp("vector", lambda e, c=c, ofs=ofs: e.tensor_copy(
                    out=xsr[:, c * 512: c * 512 + 1], in_=bnd[:, ofs + c: ofs + c + 1]))
                p.comp("vector", lambda e, c=c, t0=t0: e.tensor_copy(
                    out=xsr[:, c * 512 + 1:(c + 1) * 512], in_=x_ap(c, t0, R - 1)))

    def save_bnd(t0, rg, which):
        par = (rg + 1) % 2
        ofs = which * 16 + par * 8

        def cp(eng, t0=t0, ofs=ofs):
            return eng.tensor_copy(
                out=bnd[:, ofs:ofs + 8],
                in_=xI[:].rearrange("p (c t) -> p c t", c=8)[:, :, 1 + t0 + R - 1: 1 + t0 + R])

        p.comp("vector", cp)

    ws = getattr(ctx, "wsrc", dram)
    wqueue = getattr(ctx, "wqueue", lambda: "gpsimd")
    W = ws[f"L1f_Wqkv_{li}"]
    WO = ws[f"L1f_Wo_{li}"]
    W1 = ws[f"L1f_W1_{li}"]
    W2 = ws[f"L1f_W2_{li}"]

    with ctx.scope():
        p.tag("L1f_attn")
        lnr = ctx.sbuf("lnr", [128, 8 * 512], BF16)
        xbr = lnr
        qfin = ctx.sbuf("qfin", [128, 8 * 512], BF16)
        kfin = ctx.sbuf("kfin", [128, 8 * 512], BF16)
        kprev = ctx.sbuf("kprev", [128, 8 * 256], BF16)
        vtl = ctx.sbuf("vtl", [128, 4 * 1040], BF16)
        vprev = ctx.sbuf("vprev", [128, 2 * 1040], BF16)
        ofT = ctx.sbuf("ofT2", [128, 8 * 512], BF16)
        expPf = ctx.sbuf("expP", [128, 4 * 256], BF16)
        rlbf = ctx.sbuf("rlb", [128, 4 * 256], F32)
        ropes = ctx.sbuf("ropes2", [128, 4 * 512], BF16)

        def vones(eng, vtl=vtl, vprev=vprev):
            eng.memset(vtl[:].rearrange("p (a b) -> p a b", b=65)[:, :, 64:65], 1.0)
            return eng.memset(vprev[:].rearrange("p (a b) -> p a b", b=65)[:, :, 64:65], 1.0)

        p.comp("vector", vones)
        p.comp("vector", lambda e: e.memset(bnd[:, 0:8], 0.0))
        for rg in range(nrange):
            t0 = rg * R
            build_xsr(t0, rg, 0)
            save_bnd(t0, rg, 0)
            for c in range(8):
                p.comp("scalar", lambda e, c=c: e.copy(
                    out=xbr[:, c * R:(c + 1) * R], in_=xsr[:, c * R:(c + 1) * R]))
            for j, nm in enumerate(("csq", "ssq", "csk", "ssk")):
                p.load("ropes2", lambda e, j=j, nm=nm, t0=t0, ropes=ropes: e.dma_start(
                    out=ropes[:, j * 512:(j + 1) * 512], in_=dram[nm][:, t0:t0 + R]),
                    queue="sync")
            for qk in range(2):
                dst = qfin if qk == 0 else kfin
                WB = wbig if qk == 0 else wbig2
                for wh in range(2):
                    src_w = W.rearrange("(c p) n -> p c n", p=128)[
                        :, :, qk * 1024 + wh * 512: qk * 1024 + (wh + 1) * 512]
                    dst_w = WB[:, wh * 4096:(wh + 1) * 4096].rearrange(
                        "p (c n) -> p c n", c=8)
                    p.load("wbig", lambda e, a=dst_w, s_=src_w: e.dma_start(out=a, in_=s_),
                           queue=wqueue())
                for ch in range(8):
                    pq = ctx.bank()
                    qsc = qscf[:, (ch % 2) * 512:(ch % 2) * 512 + 512]
                    tsc2 = tsc2f[:, (ch % 2) * 512:(ch % 2) * 512 + 512]
                    rnorm2 = rnorm2f[:, (ch % 2) * 512:(ch % 2) * 512 + 512]

                    def qmm(eng, ch=ch, pq=pq, xbr=xbr, WB=WB):
                        base = (ch // 4) * 4096 + (ch % 4) * 128
                        last = None
                        for mc in range(8):
                            last = eng.matmul(
                                pq[:, :], WB[:, base + mc * 512: base + mc * 512 + 128],
                                xbr[:, mc * R:(mc + 1) * R], start=(mc == 0), stop=(mc == 7))
                        return last

                    p.comp("tensor", qmm)
                    p.comp("scalar", lambda e, pq=pq, qsc=qsc: e.activation(
                        out=qsc[:], in_=pq[:, :], func=AF.Square))
                    pr_ = ctx.bank()
                    p.comp("tensor", lambda e, pr_=pr_, qsc=qsc: e.matmul(
                        pr_[0:2, :], hsel[:, 0:2], qsc[:], start=True, stop=True))
                    p.comp("vector", lambda e, pr_=pr_, rnorm2=rnorm2: e.tensor_scalar(
                        out=rnorm2[:, :], in0=pr_[0:2, :], scalar1=1e-24, scalar2=None,
                        op0=ALU.max))
                    p.comp("scalar", lambda e, rnorm2=rnorm2: e.activation(
                        out=rnorm2[:, :], in_=rnorm2[:, :], func=AF.Ln))
                    p.comp("scalar", lambda e, rnorm2=rnorm2: e.activation(
                        out=rnorm2[:, :], in_=rnorm2[:, :], func=AF.Exp, scale=-0.5))
                    pb2 = ctx.bank()
                    p.comp("tensor", lambda e, pb2=pb2, qk=qk, rnorm2=rnorm2: e.matmul(
                        pb2[:, :], qsind[:, (li * 2 + qk) * 128:(li * 2 + qk + 1) * 128],
                        rnorm2[:, :], start=True, stop=True))
                    p.comp("vector", lambda e, pb2=pb2, tsc2=tsc2: e.tensor_copy(out=tsc2[:], in_=pb2[:, :]))
                    p.comp("vector", lambda e, pq=pq, qsc=qsc, tsc2=tsc2: e.tensor_mul(
                        out=qsc[:], in0=pq[:, :], in1=tsc2[:]))
                    pr2 = ctx.bank()
                    p.comp("tensor", lambda e, pr2=pr2, qsc=qsc: e.matmul(
                        pr2[:, :], rperm[:], qsc[:], start=True, stop=True))
                    co = 0 if qk == 0 else 1024
                    p.comp("vector", lambda e, ch=ch, dst=dst, co=co, ropes=ropes, qsc=qsc: e.tensor_mul(
                        out=dst[:, ch * R:(ch + 1) * R], in0=qsc[:],
                        in1=ropes[:, co:co + 512]))
                    p.comp("vector", lambda e, pr2=pr2, co=co, ropes=ropes, tsc2=tsc2: e.tensor_mul(
                        out=tsc2[:], in0=pr2[:, :], in1=ropes[:, co + 512:co + 1024]))
                    p.comp("vector", lambda e, ch=ch, dst=dst, tsc2=tsc2: e.tensor_add(
                        out=dst[:, ch * R:(ch + 1) * R], in0=dst[:, ch * R:(ch + 1) * R],
                        in1=tsc2[:]))
            for vb in range(2):
                src_w = W.rearrange("(c p) n -> p c n", p=128)[
                    :, :, 2048 + vb * 512: 2048 + (vb + 1) * 512]
                dst_w = wbig[:, vb * 4096:(vb + 1) * 4096].rearrange(
                    "p (c n) -> p c n", c=8)
                p.load("wbig", lambda e, a=dst_w, s_=src_w: e.dma_start(out=a, in_=s_),
                       queue=wqueue())
            for tt in range(4):
                for vb in range(2):
                    pv = ctx.bank()

                    def vmm(eng, tt=tt, vb=vb, pv=pv, xbr=xbr):
                        last = None
                        for mc in range(8):
                            last = eng.matmul(
                                pv[:, :],
                                xbr[:, mc * R + tt * 128: mc * R + (tt + 1) * 128],
                                wbig[:, vb * 4096 + mc * 512: vb * 4096 + (mc + 1) * 512],
                                start=(mc == 0), stop=(mc == 7))
                        return last

                    p.comp("tensor", vmm)
                    vo = vtl[:, tt * 1040 + vb * 8 * 65: tt * 1040 + vb * 8 * 65 + 8 * 65]
                    p.comp("vector", lambda e, pv=pv, vo=vo: e.tensor_copy(
                        out=vo.rearrange("p (h n) -> p h n", h=8)[:, :, 0:64],
                        in_=pv[:, :].rearrange("p (h n) -> p h n", h=8)))
            for wl in range(2):
                w_glob = rg * 2 + wl
                qc0 = wl * 256
                for h in range(16):
                    hc = h // 2
                    ro = (h % 2) * 64
                    po = ctx.reserve()
                    rlb = rlbf[:, (h % 4) * 256:(h % 4) * 256 + 256]
                    nkc_list = [r for r in range(4) if not (w_glob == 0 and r < 2)]
                    for ki, r in enumerate(nkc_list):
                        expP = expPf[:, (ki % 4) * 256:(ki % 4) * 256 + 256]
                        if wl == 0:
                            if r < 2:
                                kap = kprev[ro:ro + 64, hc * 256 + r * 128: hc * 256 + (r + 1) * 128]
                            else:
                                kap = kfin[ro:ro + 64, hc * R + (r - 2) * 128: hc * R + (r - 1) * 128]
                        else:
                            kap = kfin[ro:ro + 64, hc * R + (r - 2) * 128 + 256: hc * R + (r - 1) * 128 + 256]
                        ps_ = ctx.bank()
                        p.comp("tensor", lambda e, ps_=ps_, kap=kap, hc=hc, ro=ro, qc0=qc0, qfin=qfin: e.matmul(
                            ps_[:, 0:256], kap,
                            qfin[ro:ro + 64, hc * R + qc0: hc * R + qc0 + 256],
                            start=True, stop=True))
                        p.comp("scalar", lambda e, ps_=ps_, expP=expP: e.activation(
                            out=expP[:], in_=ps_[:, 0:256], func=AF.Exp))
                        p.comp("vector", lambda e, r=r, expP=expP: e.tensor_mul(
                            out=expP[:], in0=expP[:], in1=lmask[:, r * 256:(r + 1) * 256]))
                        gt = 2 * w_glob - 2 + r
                        lt = gt - rg * 4
                        if lt < 0:
                            vap = vprev[:, (lt + 2) * 1040 + h * 65:(lt + 2) * 1040 + h * 65 + 65]
                        else:
                            vap = vtl[:, lt * 1040 + h * 65: lt * 1040 + h * 65 + 65]
                        p.comp("tensor", lambda e, po=po, vap=vap, ki=ki, n=len(nkc_list), expP=expP: e.matmul(
                            po[0:65, 0:256], vap, expP[:],
                            start=(ki == 0), stop=(ki == n - 1)))
                    ctx.release(po)
                    p.comp("scalar", lambda e, po=po, rlb=rlb: e.activation(
                        out=rlb[0:1, :], in_=po[64:65, 0:256], func=AF.Ln))
                    p.comp("scalar", lambda e, rlb=rlb: e.activation(
                        out=rlb[0:1, :], in_=rlb[0:1, :], func=AF.Exp, scale=-1.0))
                    pb3 = ctx.bank()
                    p.comp("tensor", lambda e, pb3=pb3, rlb=rlb: e.matmul(
                        pb3[0:64, 0:256], b["ones_row"][0:1, 0:64], rlb[0:1, :],
                        start=True, stop=True))
                    p.comp("vector", lambda e, pb3=pb3, rlb=rlb: e.tensor_copy(
                        out=rlb[0:64, :], in_=pb3[0:64, 0:256]))
                    p.comp("vector", lambda e, po=po, hc=hc, ro=ro, qc0=qc0, ofT=ofT, rlb=rlb: e.tensor_mul(
                        out=ofT[ro:ro + 64, hc * R + qc0: hc * R + qc0 + 256],
                        in0=po[0:64, 0:256], in1=rlb[0:64, :]))
            p.comp("vector", lambda e, kprev=kprev, kfin=kfin: e.tensor_copy(
                out=kprev[:].rearrange("p (c n) -> p c n", c=8),
                in_=kfin[:].rearrange("p (c n) -> p c n", c=8)[:, :, 256:512]))
            p.comp("vector", lambda e, vprev=vprev, vtl=vtl: e.tensor_copy(
                out=vprev[:], in_=vtl[:, 2 * 1040:4 * 1040]))
            for wh in range(2):
                src_w = WO.rearrange("(c p) n -> p c n", p=128)[
                    :, :, wh * 512:(wh + 1) * 512]
                dst_w = wbig2[:, wh * 4096:(wh + 1) * 4096].rearrange(
                    "p (c n) -> p c n", c=8)
                p.load("wbig", lambda e, a=dst_w, s_=src_w: e.dma_start(out=a, in_=s_),
                       queue=wqueue())
            for dc in range(8):
                pw = ctx.bank()

                def wmm(eng, pw=pw, dc=dc, ofT=ofT):
                    base = (dc // 4) * 4096 + (dc % 4) * 128
                    last = None
                    for ic in range(8):
                        last = eng.matmul(pw[:, :], wbig2[:, base + ic * 512: base + ic * 512 + 128],
                                          ofT[:, ic * R:(ic + 1) * R],
                                          start=(ic == 0), stop=(ic == 7))
                    return last

                p.comp("tensor", wmm)
                p.comp("vector", lambda e, dc=dc, pw=pw, t0=t0: e.tensor_add(
                    out=x_ap(dc, t0, R), in0=x_ap(dc, t0, R), in1=pw[:, :]))

    with ctx.scope():
        p.tag("L1f_ffn")
        lnrf = ctx.sbuf("lnrf", [128, 8 * T], BF16)
        hblk = ctx.sbuf("hblk", [128, 4 * 512], BF16)
        gelu_sf = ctx.sbuf("gelu_s", [128, 2 * 512], BF16)
        w2b = ctx.sbuf("w2b", [128, 4 * 1024], BF16)
        w2b2 = ctx.sbuf("w2b2", [128, 4 * 1024], BF16)
        for rg in range(nrange):
            t0 = rg * R
            emit_ln_fm(ctx, None, R, None,
                       g_col=lambda c: lngc[:, li * 8 + c: li * 8 + c + 1],
                       b_col=lambda c: lnbc[:, li * 8 + c: li * 8 + c + 1],
                       xc_fn=lambda c, t0=t0: xs_ap(c, t0, R),
                       lc_fn=lambda c, t0=t0, lnrf=lnrf: lnrf[:, c * T + t0: c * T + t0 + R])
        for bi_, blk in enumerate(OC_BLOCKS):
            bofs = blk[0][0]
            bw = sum(sz for _, sz in blk)
            wb1 = wbig if bi_ % 2 == 0 else wbig2
            for h in range((bw + 255) // 256):
                hw = min(256, bw - h * 256)
                for part, ofs0 in ((2730, 4096), (0, 0)):
                    src = W1.rearrange("(c p) n -> p c n", p=128)[
                        :, :, part + bofs + h * 256: part + bofs + h * 256 + hw]
                    dsl = wb1[:, ofs0 + h * 2048: ofs0 + h * 2048 + 8 * 256].rearrange(
                        "p (c n) -> p c n", c=8)[:, :, 0:hw]
                    p.load("wbig", lambda e, a=dsl, s_=src: e.dma_start(out=a, in_=s_),
                           queue=wqueue())
            wb = w2b if bi_ % 2 == 0 else w2b2
            for ci, (cofs, csz) in enumerate(blk):
                p.load("w2bl", lambda e, ci=ci, cofs=cofs, csz=csz, wb=wb: e.dma_start(
                    out=wb[0:csz, ci * 1024:(ci + 1) * 1024], in_=W2[cofs:cofs + csz, :]),
                    queue=wqueue())
            for tt in range(nrange):
                t0 = tt * R
                for ci, (cofs, csz) in enumerate(blk):
                    cloc = cofs - bofs
                    ach = cofs // 128
                    wbase = (cloc // 256) * 2048 + (cloc % 256)
                    pg = ctx.bank()
                    gelu_s = gelu_sf[:, (ci % 2) * 512:(ci % 2) * 512 + 512]

                    def gmm(eng, pg=pg, wbase=wbase, csz=csz, t0=t0, lnrf=lnrf, wb1=wb1):
                        last = None
                        for mc in range(8):
                            last = eng.matmul(
                                pg[0:csz, :], wb1[:, 4096 + wbase + mc * 256: 4096 + wbase + mc * 256 + csz],
                                lnrf[:, mc * T + t0: mc * T + t0 + R], start=(mc == 0), stop=(mc == 7))
                        return last

                    p.comp("tensor", gmm)
                    p.comp("scalar", lambda e, pg=pg, csz=csz, ach=ach, gelu_s=gelu_s: e.activation(
                        out=gelu_s[0:csz, :], in_=pg[0:csz, :], func=AF.Gelu,
                        bias=b1gc[0:csz, li * 22 + ach: li * 22 + ach + 1]))
                    pa = ctx.bank()

                    def ammf(eng, pa=pa, wbase=wbase, csz=csz, t0=t0, lnrf=lnrf, wb1=wb1):
                        last = None
                        for mc in range(8):
                            last = eng.matmul(
                                pa[0:csz, :], wb1[:, wbase + mc * 256: wbase + mc * 256 + csz],
                                lnrf[:, mc * T + t0: mc * T + t0 + R], start=(mc == 0), stop=(mc == 7))
                        return last

                    p.comp("tensor", ammf)
                    p.comp("vector", lambda e, pa=pa, csz=csz, ach=ach, ci=ci, hblk=hblk, gelu_s=gelu_s: e.scalar_tensor_tensor(
                        out=hblk[0:csz, ci * 512:(ci + 1) * 512], in0=pa[0:csz, :],
                        scalar=b1ac[0:csz, li * 22 + ach: li * 22 + ach + 1],
                        in1=gelu_s[0:csz, :], op0=ALU.add, op1=ALU.mult))
                for dc in range(8):
                    pd = ctx.bank()

                    def dmm(eng, pd=pd, dc=dc, blk=blk, wb=wb, hblk=hblk):
                        last = None
                        for ci, (cofs, csz) in enumerate(blk):
                            last = eng.matmul(
                                pd[:, :], wb[0:csz, ci * 1024 + dc * 128: ci * 1024 + dc * 128 + 128],
                                hblk[0:csz, ci * 512:(ci + 1) * 512],
                                start=(ci == 0), stop=(ci == len(blk) - 1))
                        return last

                    p.comp("tensor", dmm)
                    if bi_ == 0:
                        p.comp("vector", lambda e, pd=pd, dc=dc, t0=t0: e.scalar_tensor_tensor(
                            out=x_ap(dc, t0, R), in0=pd[:, :],
                            scalar=b2c[:, li * 8 + dc: li * 8 + dc + 1], in1=x_ap(dc, t0, R),
                            op0=ALU.add, op1=ALU.add))
                    else:
                        p.comp("vector", lambda e, pd=pd, dc=dc, t0=t0: e.tensor_add(
                            out=x_ap(dc, t0, R), in0=x_ap(dc, t0, R), in1=pd[:, :]))


def build_l1ffn(layers, T=T_SEQ, nseq=2, first=False, last=False):
    ph = ("l0a", "l0f", "l1a", "l1f", "mem", "cst", "tin", "spl", "ln4", "fin")
    """Launch for one or more inner L1ffn layers. If first: input is the outer
    FM x and we compute ln4 into the inner residual; else input is the inner
    residual. If last: output = outer_x + inner (token-major, transposed);
    else output is the inner residual (FM, unpadded)."""
    nc = bass.Bass("TRN2", target_bir_lowering=False)
    es = ExitStack()
    p = (Prog2 if USE_PROG2 else Prog)(nc, es)
    ctx = Ctx(nc, es, p)
    dram = _common_consts(nc, ctx, 512, with_tstage=last)
    TP = T + 1
    R = 512
    for s in range(nseq):
        if first or last:
            dram[f"xout_s{s}"] = nc.dram_tensor(f"xout_s{s}", [D_MODEL, T], F32,
                                                kind="ExternalInput").ap()
        if not first:
            dram[f"xi_in_s{s}"] = nc.dram_tensor(f"xi_in_s{s}", [D_MODEL, T], F32,
                                                 kind="ExternalInput").ap()
        if last:
            dram[f"y_s{s}"] = nc.dram_tensor(f"y_s{s}", [T, D_MODEL], F32,
                                             kind="ExternalOutput").ap()
        else:
            dram[f"xi_s{s}"] = nc.dram_tensor(f"xi_s{s}", [D_MODEL, T], F32,
                                              kind="ExternalOutput").ap()
    for li in layers:
        dram[f"L1f_Wqkv_{li}"] = nc.dram_tensor(f"L1f_Wqkv_{li}", [1024, 3072], F32, kind="ExternalInput").ap()
        dram[f"L1f_Wo_{li}"] = nc.dram_tensor(f"L1f_Wo_{li}", [1024, 1024], F32, kind="ExternalInput").ap()
        dram[f"L1f_W1_{li}"] = nc.dram_tensor(f"L1f_W1_{li}", [1024, 5460], F32, kind="ExternalInput").ap()
        dram[f"L1f_W2_{li}"] = nc.dram_tensor(f"L1f_W2_{li}", [2730, 1024], F32, kind="ExternalInput").ap()
    for nm in ("csq", "ssq", "csk", "ssk"):
        dram[nm] = nc.dram_tensor(nm, [128, T], BF16, kind="ExternalInput").ap()
    for nm, pdim, w, dt in (("rperm", 128, 128, BF16), ("lmask", 128, 4 * 256, BF16),
                            ("hsel", 128, 2, BF16), ("qsind", 2, 12 * 128, BF16),
                            ("lngc", 128, 8 * 6, F32), ("lnbc", 128, 8 * 6, F32),
                            ("b1ac", 128, 22 * 6, F32), ("b1gc", 128, 22 * 6, F32),
                            ("b2c", 128, 8 * 6, F32), ("ident_bf", 128, 128, BF16)):
        dram[nm] = nc.dram_tensor("d_" + nm, [pdim, w], dt, kind="ExternalInput").ap()
        ctx.sbuf(nm, [pdim, w], dt)
        p.load("consts", lambda e, nm=nm: e.dma_start(out=ctx.bufs[nm][:], in_=dram[nm][:]),
               queue="sync")
    ctx.dram = dram

    xI = ctx.sbuf("xI", [128, 8 * TP], F32)
    ctx.sbuf("wbig", [128, 8 * 1024], BF16)
    ctx.sbuf("wbig2", [128, 8 * 1024], BF16)
    ctx.sbuf("lnr", [128, 8 * 512], BF16)
    ctx.sbuf("qsc", [128, 2 * 512], BF16)
    ctx.sbuf("tsc2", [128, 2 * 512], BF16)
    ctx.sbuf("rnorm2", [2, 2 * 512], BF16)
    xor_ = ctx.sbuf("xoutr", [128, 8 * 512], F32)
    ctx.sbuf("bnd", [128, 32], F32)

    for s in range(nseq):
        def pads(eng):
            eng.memset(xI[:].rearrange("p (c t) -> p c t", c=8)[:, :, 0:1], 0.0)
            eng.memset(ctx.bufs["vtl"][:].rearrange("p (a b) -> p a b", b=65)[:, :, 64:65], 1.0)
            return eng.memset(ctx.bufs["vprev"][:].rearrange("p (a b) -> p a b", b=65)[:, :, 64:65], 1.0)

        p.comp("vector", pads, needs=("consts",), wait_stores=True)
        if first:
            for rg in range(T // R):
                t0 = rg * R
                for c in range(8):
                    p.load(f"xo{c % 2}", lambda e, c=c, s=s, t0=t0, xor_=xor_: e.dma_start(
                        out=xor_[:, c * R:(c + 1) * R],
                        in_=dram[f"xout_s{s}"][c * 128:(c + 1) * 128, t0:t0 + R]),
                        queue="sync", gate_stores=True)
                p.comp("vector", lambda e: e.engine_nop(), needs=("xo0", "xo1"))
                lnr_b = ctx.bufs["lnr"]
                emit_ln_fm(ctx, None, R, None,
                           xc_fn=lambda c, xor_=xor_: xor_[:, c * R:(c + 1) * R],
                           lc_fn=lambda c, t0=t0, xI=xI: xI[:, c * TP + 1 + t0: c * TP + 1 + t0 + R],
                           cast_fn=lambda c, lnr_b=lnr_b: lnr_b[:, c * R:(c + 1) * R])
                p.mark("xo0")
                p.mark("xo1")
        else:
            for c in range(8):
                p.load(f"xo{c % 2}", lambda e, c=c, s=s, xI=xI: e.dma_start(
                    out=xI[:, c * TP + 1: c * TP + 1 + T],
                    in_=dram[f"xi_in_s{s}"][c * 128:(c + 1) * 128, :]),
                    queue="sync", gate_stores=True)
            p.comp("vector", lambda e: e.engine_nop(), needs=("xo0", "xo1"))
        for li in layers:
            emit_l1ffn_layer(ctx, T, xI, li)
        if last:
            ident = ctx.bufs["ident"]
            stage = ctx.bufs["tstage"][:, 0:1024]
            p.mark("xo0")
            p.mark("xo1")
            for rg in range((T // R) if ("fin" in ph or "l1f" in ph) else 0):
                t0 = rg * R
                for c in range(8):
                    p.load(f"xo{c % 2}", lambda e, c=c, s=s, t0=t0, xor_=xor_: e.dma_start(
                        out=xor_[:, c * R:(c + 1) * R],
                        in_=dram[f"xout_s{s}"][c * 128:(c + 1) * 128, t0:t0 + R]),
                        queue="sync", gate_stores=True)
                p.comp("vector", lambda e: e.engine_nop(), needs=("xo0", "xo1"))
                for c in range(8):
                    p.comp("vector", lambda e, c=c, t0=t0, xor_=xor_, xI=xI: e.tensor_add(
                        out=xor_[:, c * R:(c + 1) * R], in0=xor_[:, c * R:(c + 1) * R],
                        in1=xI[:, c * TP + 1 + t0: c * TP + 1 + t0 + R]))
                for tt in range(R // 128):
                    for g in range(2):
                        pb = ctx.bank()

                        def tr(eng, g=g, tt=tt, pb=pb):
                            last_i = None
                            for i in range(4):
                                c = g * 4 + i
                                last_i = eng.matmul(
                                    pb[:, i * 128:(i + 1) * 128],
                                    xor_[:, c * R + tt * 128: c * R + (tt + 1) * 128],
                                    ident[:], start=(i == 0), stop=(i == 3),
                                    is_transpose=True)
                            return last_i

                        p.comp("tensor", tr)

                        def ev(eng, g=g, pb=pb):
                            last_i = None
                            for i in range(4):
                                c = g * 4 + i
                                last_i = eng.tensor_copy(
                                    out=stage[:, c * 128:(c + 1) * 128],
                                    in_=pb[:, i * 128:(i + 1) * 128])
                            return last_i

                        p.comp("vector", ev, wait_stores=(g == 0))
                    p.store(lambda e, s=s, t0=t0, tt=tt: e.dma_start(
                        out=dram[f"y_s{s}"][t0 + tt * 128: t0 + (tt + 1) * 128, :],
                        in_=stage[:]))
                p.mark("xo0")
                p.mark("xo1")
        else:
            for c in range(8):
                p.store(lambda e, c=c, s=s, xI=xI: e.dma_start(
                    out=dram[f"xi_s{s}"][c * 128:(c + 1) * 128, :],
                    in_=xI[:, c * TP + 1: c * TP + 1 + T]))
    p.finish()
    p.emit()
    nc.finalize()
    return nc, es, {"const_vals": dict(ctx.const_vals)}


def l1ffn_const_inputs(T=T_SEQ):
    import ml_dtypes
    out = {}
    DH = 64
    W = 256
    half = np.arange(0, DH, 2, dtype=np.float64)
    inv_freq = 1.0 / (10000.0 ** (half / DH))
    t = np.arange(T, dtype=np.float64)
    fr = t[:, None] * inv_freq[None, :]
    fr = np.concatenate([fr, fr], -1)
    base_scale = (half + 0.4 * DH) / (1.4 * DH)
    power = (t - T // 2) / (W // 2)
    scale = base_scale[None, :] ** power[:, None]
    scale = np.concatenate([scale, scale], -1)
    cos = np.cos(fr); sin = np.sin(fr)
    csq = (cos * scale * 8.0).T
    ssq = (sin * scale * 8.0).T
    csk = (cos / scale).T
    ssk = (sin / scale).T
    for nm, v in (("csq", csq), ("ssq", ssq), ("csk", csk), ("ssk", ssk)):
        out[nm] = np.tile(v, (2, 1)).astype(ml_dtypes.bfloat16)
    R64 = np.zeros((64, 64))
    for c in range(32):
        R64[c, c + 32] = -1.0
        R64[c + 32, c] = 1.0
    Rm = np.zeros((128, 128))
    Rm[0:64, 0:64] = R64
    Rm[64:128, 64:128] = R64
    out["rperm"] = np.ascontiguousarray(Rm.T).astype(ml_dtypes.bfloat16)
    i = np.arange(128)[:, None]
    j = np.arange(256)[None, :]
    lm = np.zeros((128, 4 * 256))
    for r in range(4):
        m = ((i + 128 * r >= j) & (128 * r + i <= 256 + j)).astype(np.float64)
        lm[:, r * 256:(r + 1) * 256] = m
    out["lmask"] = lm.astype(ml_dtypes.bfloat16)
    hs = np.zeros((128, 2))
    hs[0:64, 0] = 1.0
    hs[64:128, 1] = 1.0
    out["hsel"] = hs.astype(ml_dtypes.bfloat16)
    out["ident_bf"] = np.eye(128).astype(ml_dtypes.bfloat16)
    return out


def l1ffn_param_inputs(q_scale, k_scale, ln_g, ln_b, b1, b2):
    import ml_dtypes
    qsind = np.zeros((2, 12 * 128), dtype=np.float64)
    for li in range(6):
        for qk, arr in ((0, q_scale), (1, k_scale)):
            col0 = (li * 2 + qk) * 128
            qsind[0, col0:col0 + 64] = np.asarray(arr)[li]
            qsind[1, col0 + 64:col0 + 128] = np.asarray(arr)[li]
    lngc = np.zeros((128, 48), np.float32)
    lnbc = np.zeros((128, 48), np.float32)
    b2c = np.zeros((128, 48), np.float32)
    b1ac = np.zeros((128, 132), np.float32)
    b1gc = np.zeros((128, 132), np.float32)
    for li in range(6):
        for c in range(8):
            lngc[:, li * 8 + c] = np.asarray(ln_g)[li][c * 128:(c + 1) * 128]
            lnbc[:, li * 8 + c] = np.asarray(ln_b)[li][c * 128:(c + 1) * 128]
            b2c[:, li * 8 + c] = np.asarray(b2)[li][c * 128:(c + 1) * 128]
        for ai, (aofs, asz) in enumerate(A_CHUNKS):
            b1ac[0:asz, li * 22 + ai] = np.asarray(b1)[li][aofs:aofs + asz]
            b1gc[0:asz, li * 22 + ai] = np.asarray(b1)[li][2730 + aofs:2730 + aofs + asz]
    return {"qsind": qsind.astype(ml_dtypes.bfloat16), "lngc": lngc, "lnbc": lnbc,
            "b1ac": b1ac, "b1gc": b1gc, "b2c": b2c}



_PROGRAMS = {}


def _get_program(key, builder):
    if key not in _PROGRAMS:
        _PROGRAMS[key] = builder()
    return _PROGRAMS[key]


def kernel(**inputs):
    import ml_dtypes
    T = T_SEQ
    x = np.asarray(inputs["L0_attn_x"])
    ncore = N_CORES

    l0w = {
        "WqT": np.ascontiguousarray(np.asarray(inputs["L0_attn_Wq"]).T),
        "WkT": np.ascontiguousarray(np.asarray(inputs["L0_attn_Wk"]).T),
        "WvT": np.ascontiguousarray(np.asarray(inputs["L0_attn_Wv"]).T),
        "WoT": np.ascontiguousarray(np.asarray(inputs["L0_attn_Wo"]).T),
    }
    moew = l0ffn_weight_inputs(inputs["L0_ffn_Wr"], inputs["L0_ffn_A"], inputs["L0_ffn_Bw"])
    l1aw = {
        "Wq": np.ascontiguousarray(np.asarray(inputs["L1_attn_Wq"])),
        "Wk": np.ascontiguousarray(np.asarray(inputs["L1_attn_Wk"])),
        "Wv": np.ascontiguousarray(np.asarray(inputs["L1_attn_Wv"])),
        "Wg": np.ascontiguousarray(np.asarray(inputs["L1_attn_Wg"])),
        "Wo": np.ascontiguousarray(np.asarray(inputs["L1_attn_Wo"])),
        "gnb": np.broadcast_to(np.asarray(inputs["L1_attn_gn_weight"]), (128, 512)).astype(ml_dtypes.bfloat16),
    }
    l1fw = {}
    for li in range(6):
        l1fw[f"L1f_Wqkv_{li}"] = np.ascontiguousarray(np.asarray(inputs["L1_ffn_Wqkv"])[li])
        l1fw[f"L1f_Wo_{li}"] = np.ascontiguousarray(np.asarray(inputs["L1_ffn_Wo"])[li])
        l1fw[f"L1f_W1_{li}"] = np.ascontiguousarray(np.asarray(inputs["L1_ffn_W1"])[li])
        l1fw[f"L1f_W2_{li}"] = np.ascontiguousarray(np.asarray(inputs["L1_ffn_W2"])[li])
    l1fp = l1ffn_param_inputs(inputs["L1_ffn_q_scale"], inputs["L1_ffn_k_scale"],
                              inputs["L1_ffn_ln_g"], inputs["L1_ffn_ln_b"],
                              inputs["L1_ffn_b1"], inputs["L1_ffn_b2"])

    ncf, _, mf = _get_program("fused", lambda: build_twopass(lambda: build_fused(T=T, nseq=2)))
    cc = const_inputs(T)
    c3 = l1attn_const_inputs(T)
    cf = l1ffn_const_inputs(T)
    base = dict(cc)
    base["cpool"] = cpool_array(mf["const_vals"])
    base["cmask"] = l0attn_const_inputs(T)["cmask"]
    for k in ("dt_t", "cqb", "ckb", "ident_bf"):
        base[k] = c3[k]
    for k in ("rperm", "lmask", "hsel"):
        base[k] = cf[k]
    base["gnb"] = l1aw["gnb"]
    for k, v in l1fp.items():
        base[k] = v
    base = {("d_" + k): v for k, v in base.items()}
    for k in ("cosq", "sinq", "cosk", "sink"):
        base[k] = c3[k]
    for k in ("csq", "ssq", "csk", "ssk"):
        base[k] = cf[k]
    base.update(l0w)
    base.update(moew)
    for k in ("Wq", "Wk", "Wv", "Wg", "Wo"):
        base[k] = l1aw[k]
    base.update(l1fw)
    ims = []
    for c in range(ncore):
        im = dict(base)
        im["x_s0"] = np.ascontiguousarray(x[2 * c])
        im["x_s1"] = np.ascontiguousarray(x[2 * c + 1])
        ims.append(im)
    rf = run_bass_kernel_spmd(ncf, ims, core_ids=list(range(ncore)))
    out = np.empty((16, T, D_MODEL), dtype=np.float32)
    for c in range(ncore):
        out[2 * c] = rf.results[c]["y_s0"]
        out[2 * c + 1] = rf.results[c]["y_s1"]
    return out


def kernel_multilaunch(**inputs):
    import ml_dtypes
    T = T_SEQ
    x = np.asarray(inputs["L0_attn_x"])
    ncore = N_CORES
    l0w = {
        "WqT": np.ascontiguousarray(np.asarray(inputs["L0_attn_Wq"]).T),
        "WkT": np.ascontiguousarray(np.asarray(inputs["L0_attn_Wk"]).T),
        "WvT": np.ascontiguousarray(np.asarray(inputs["L0_attn_Wv"]).T),
        "WoT": np.ascontiguousarray(np.asarray(inputs["L0_attn_Wo"]).T),
    }
    moew = l0ffn_weight_inputs(inputs["L0_ffn_Wr"], inputs["L0_ffn_A"], inputs["L0_ffn_Bw"])
    l1aw = {
        "Wq": np.ascontiguousarray(np.asarray(inputs["L1_attn_Wq"])),
        "Wk": np.ascontiguousarray(np.asarray(inputs["L1_attn_Wk"])),
        "Wv": np.ascontiguousarray(np.asarray(inputs["L1_attn_Wv"])),
        "Wg": np.ascontiguousarray(np.asarray(inputs["L1_attn_Wg"])),
        "Wo": np.ascontiguousarray(np.asarray(inputs["L1_attn_Wo"])),
        "gnb": np.broadcast_to(np.asarray(inputs["L1_attn_gn_weight"]), (128, 512)).astype(ml_dtypes.bfloat16),
    }
    l1fw = {}
    for li in range(6):
        l1fw[f"L1f_Wqkv_{li}"] = np.ascontiguousarray(np.asarray(inputs["L1_ffn_Wqkv"])[li])
        l1fw[f"L1f_Wo_{li}"] = np.ascontiguousarray(np.asarray(inputs["L1_ffn_Wo"])[li])
        l1fw[f"L1f_W1_{li}"] = np.ascontiguousarray(np.asarray(inputs["L1_ffn_W1"])[li])
        l1fw[f"L1f_W2_{li}"] = np.ascontiguousarray(np.asarray(inputs["L1_ffn_W2"])[li])
    l1fp = l1ffn_param_inputs(inputs["L1_ffn_q_scale"], inputs["L1_ffn_k_scale"],
                              inputs["L1_ffn_ln_g"], inputs["L1_ffn_ln_b"],
                              inputs["L1_ffn_b1"], inputs["L1_ffn_b2"])

    nc1, _, m1 = _get_program("l0attn", lambda: build_twopass(lambda: build_l0attn(T=T, nseq=2)))
    nc2, _, m2 = _get_program("l0ffn", lambda: build_twopass(lambda: build_l0ffn(T=T, nseq=2)))
    nc3, _, m3 = _get_program("l1attn", lambda: build_twopass(lambda: build_l1attn(T=T, nseq=2)))
    l1fall = _get_program("l1fall", lambda: build_twopass(lambda: build_l1ffn(
        list(range(6)), T=T, nseq=2, first=True, last=True)))

    cc = const_inputs(T)

    def cmaps(meta, extra):
        m = dict(cc)
        m.update(extra)
        m["cpool"] = cpool_array(meta["const_vals"])
        return {("d_" + k): v for k, v in m.items()}

    base1 = cmaps(m1, {"cmask": l0attn_const_inputs(T)["cmask"]})
    base1.update(l0w)
    ims = []
    for c in range(ncore):
        im = dict(base1)
        im["x_s0"] = np.ascontiguousarray(x[2 * c])
        im["x_s1"] = np.ascontiguousarray(x[2 * c + 1])
        ims.append(im)
    r1 = run_bass_kernel_spmd(nc1, ims, core_ids=list(range(ncore)))
    xfm = [[r1.results[c][f"xfm_s{s}"] for s in range(2)] for c in range(ncore)]

    base2 = cmaps(m2, {})
    base2.update(moew)
    ims = []
    for c in range(ncore):
        im = dict(base2)
        im["xfm_in_s0"] = xfm[c][0]
        im["xfm_in_s1"] = xfm[c][1]
        ims.append(im)
    r2 = run_bass_kernel_spmd(nc2, ims, core_ids=list(range(ncore)))
    xfm = [[r2.results[c][f"xfm_s{s}"] for s in range(2)] for c in range(ncore)]

    c3 = l1attn_const_inputs(T)
    base3 = cmaps(m3, {k: c3[k] for k in ("dt_t", "cqb", "ckb", "ident_bf")})
    base3["d_gnb"] = l1aw["gnb"]
    for k in ("cosq", "sinq", "cosk", "sink"):
        base3[k] = c3[k]
    for k in ("Wq", "Wk", "Wv", "Wg", "Wo"):
        base3[k] = l1aw[k]
    ims = []
    for c in range(ncore):
        im = dict(base3)
        im["xfm_in_s0"] = xfm[c][0]
        im["xfm_in_s1"] = xfm[c][1]
        ims.append(im)
    r3 = run_bass_kernel_spmd(nc3, ims, core_ids=list(range(ncore)))
    xfm = [[r3.results[c][f"xfm_s{s}"] for s in range(2)] for c in range(ncore)]

    cf = l1ffn_const_inputs(T)
    ncl, _, ml = l1fall
    base = cmaps(ml, {k: cf[k] for k in ("rperm", "lmask", "hsel", "ident_bf")})
    for k in ("csq", "ssq", "csk", "ssk"):
        base[k] = cf[k]
    for k, v in l1fp.items():
        base["d_" + k] = v
    base.update(l1fw)
    ims = []
    for c in range(ncore):
        im = dict(base)
        im["xout_s0"] = xfm[c][0]
        im["xout_s1"] = xfm[c][1]
        ims.append(im)
    rr = run_bass_kernel_spmd(ncl, ims, core_ids=list(range(ncore)))
    out = np.empty((16, T, D_MODEL), dtype=np.float32)
    for c in range(ncore):
        out[2 * c] = rr.results[c]["y_s0"]
        out[2 * c + 1] = rr.results[c]["y_s1"]
    return out


def build_fused(T=T_SEQ, nseq=2):
    nc = bass.Bass("TRN2", target_bir_lowering=False)
    es = ExitStack()
    p = (Prog2 if USE_PROG2 else Prog)(nc, es)
    ctx = Ctx(nc, es, p)
    dram = _common_consts(nc, ctx, 512)
    TP = T + 1
    R = 512

    for s in range(nseq):
        dram[f"x_s{s}"] = nc.dram_tensor(f"x_s{s}", [T, D_MODEL], F32, kind="ExternalInput").ap()
        dram[f"y_s{s}"] = nc.dram_tensor(f"y_s{s}", [T, D_MODEL], F32, kind="ExternalOutput").ap()
        dram[f"xout_s{s}"] = nc.dram_tensor(
            f"xsp_s{s}", [D_MODEL, T], F32,
            kind=("ExternalOutput" if globals().get("_DEBUG_SPILL") else "Internal")).ap()
    for w in ("WqT", "WkT", "WvT", "WoT"):
        dram[w] = nc.dram_tensor(w, [D_MODEL, D_MODEL], F32, kind="ExternalInput").ap()
    dram["cmask"] = nc.dram_tensor("d_cmask", [128, 4 * 512], BF16, kind="ExternalInput").ap()
    dram["WrT"] = nc.dram_tensor("WrT", [D_MODEL, 8], F32, kind="ExternalInput").ap()
    dram["AT"] = nc.dram_tensor("AT", [D_MODEL, 512], F32, kind="ExternalInput").ap()
    dram["BwTp"] = nc.dram_tensor("BwTp", [128, 4096], F32, kind="ExternalInput").ap()
    dram["Wq"] = nc.dram_tensor("Wq", [1024, 1024], F32, kind="ExternalInput").ap()
    dram["Wk"] = nc.dram_tensor("Wk", [1024, 1024], F32, kind="ExternalInput").ap()
    dram["Wv"] = nc.dram_tensor("Wv", [1024, 2048], F32, kind="ExternalInput").ap()
    dram["Wg"] = nc.dram_tensor("Wg", [1024, 2048], F32, kind="ExternalInput").ap()
    dram["Wo"] = nc.dram_tensor("Wo", [2048, 1024], F32, kind="ExternalInput").ap()
    for nm in ("cosq", "sinq", "cosk", "sink", "csq", "ssq", "csk", "ssk"):
        dram[nm] = nc.dram_tensor(nm, [128, T], BF16, kind="ExternalInput").ap()
    for li in range(6):
        dram[f"L1f_Wqkv_{li}"] = nc.dram_tensor(f"L1f_Wqkv_{li}", [1024, 3072], F32, kind="ExternalInput").ap()
        dram[f"L1f_Wo_{li}"] = nc.dram_tensor(f"L1f_Wo_{li}", [1024, 1024], F32, kind="ExternalInput").ap()
        dram[f"L1f_W1_{li}"] = nc.dram_tensor(f"L1f_W1_{li}", [1024, 5460], F32, kind="ExternalInput").ap()
        dram[f"L1f_W2_{li}"] = nc.dram_tensor(f"L1f_W2_{li}", [2730, 1024], F32, kind="ExternalInput").ap()
    const_specs = [
        ("dt_t", 128, 4 * 128, BF16), ("cqb", 128, 4 * 128, BF16),
        ("ckb", 128, 4 * 128, BF16), ("gnb", 128, 512, BF16),
        ("ident_bf", 128, 128, BF16), ("rperm", 128, 128, BF16),
        ("lmask", 128, 4 * 256, BF16), ("hsel", 128, 2, BF16),
        ("qsind", 2, 12 * 128, BF16), ("lngc", 128, 8 * 6, F32),
        ("lnbc", 128, 8 * 6, F32), ("b1ac", 128, 22 * 6, F32),
        ("b1gc", 128, 22 * 6, F32), ("b2c", 128, 8 * 6, F32),
    ]
    CONST_SHAPES = {nm: (pdim, w, dt) for nm, pdim, w, dt in const_specs}
    CONST_SHAPES["cmask"] = (128, 4 * 512, BF16)
    for nm, pdim, w, dt in const_specs:
        dram[nm] = nc.dram_tensor("d_" + nm, [pdim, w], dt, kind="ExternalInput").ap()
    _pi, _wi, _di = CONST_SHAPES["ident_bf"]
    ctx.sbuf("ident_bf", [_pi, _wi], _di)
    p.load("consts", lambda e: e.dma_start(out=ctx.bufs["ident_bf"][:], in_=dram["ident_bf"][:]),
           queue="sync")

    def load_consts(names):
        for nm in names:
            pdim, w, dt = CONST_SHAPES[nm]
            buf = ctx.sbuf(nm, [pdim, w], dt)
            p.load("consts", lambda e, nm=nm, buf=buf: e.dma_start(out=buf[:], in_=dram[nm][:]),
                   queue="sync")
    ctx.dram = dram

    TWIN_SHAPES = {"WqT": [1024, 1024], "WkT": [1024, 1024], "WvT": [1024, 1024],
                   "WoT": [1024, 1024], "AT": [1024, 512], "BwTp": [128, 4096],
                   "Wq": [1024, 1024], "Wk": [1024, 1024], "Wv": [1024, 2048],
                   "Wg": [1024, 2048], "Wo": [2048, 1024]}
    for li in range(6):
        TWIN_SHAPES[f"L1f_Wqkv_{li}"] = [1024, 3072]
        TWIN_SHAPES[f"L1f_Wo_{li}"] = [1024, 1024]
        TWIN_SHAPES[f"L1f_W1_{li}"] = [1024, 5460]
        TWIN_SHAPES[f"L1f_W2_{li}"] = [2730, 1024]
    ctx.wsrc = {nm: nc.dram_tensor(nm + "_bf", sh, BF16, kind="Internal").ap()
                for nm, sh in TWIN_SHAPES.items()}
    ctx._wq_rr = 0

    def wqueue():
        ctx._wq_rr += 1
        return "sync" if ctx._wq_rr % 2 else "gpsimd"
    ctx.wqueue = wqueue

    def precast(names):
        for nm in names:
            p.load("pc_" + nm, lambda e, nm=nm: e.dma_start(
                out=ctx.wsrc[nm][:], in_=dram[nm][:]), queue="gpsimd")

    precast(["WqT", "WkT", "WvT", "WoT", "AT", "BwTp",
             "Wq", "Wk", "Wv", "Wg", "Wo",
             "L1f_Wqkv_0", "L1f_Wo_0", "L1f_W1_0", "L1f_W2_0"])

    phases_cfg = globals().get("_FUSED_PHASES")
    for s in range(nseq):
        ph = (phases_cfg or {}).get(s, ("l0a", "l0f", "l1a", "l1f"))
        with ctx.scope():
            xT = ctx.sbuf("xT", [128, 8 * T], F32)
            with ctx.scope():
                p.tag("L0attn")
                if "cst" in ph or "l0a" in ph:
                    load_consts(["cmask"])
                lnb = ctx.sbuf("lnb", [128, 8 * T], BF16)
                ctx.sbuf("qT", [128, T], BF16)
                ctx.sbuf("kT", [128, T], BF16)
                ctx.sbuf("v0", [128, (T // 128) * 65], BF16)
                ctx.sbuf("v1", [128, (T // 128) * 65], BF16)
                ctx.sbuf("opair", [128, 8 * T], BF16)
                ctx.sbuf("expscr", [128, 4 * 512], BF16)
                ctx.sbuf("rlscr", [128, 2 * 512], F32)
                ctx.sbuf("wq3", [128, 2 * 3 * 1024], BF16)
                ctx.sbuf("wo", [128, 2 * 8 * 128], BF16)

                v0_b = ctx.bufs["v0"]
                v1_b = ctx.bufs["v1"]

                def ones_cols(eng, v0_b=v0_b, v1_b=v1_b):
                    last = None
                    for tt in range(T // 128):
                        last = eng.memset(v0_b[:, tt * 65 + 64: tt * 65 + 65], 1.0)
                    for tt in range(T // 128):
                        last = eng.memset(v1_b[:, tt * 65 + 64: tt * 65 + 65], 1.0)
                    return last

                if "mem" in ph or "l0a" in ph:
                    p.comp("vector", ones_cols, needs=("consts",), wait_stores=True)
                if "tin" in ph or "l0a" in ph:
                    emit_transpose_in(ctx, dram[f"x_s{s}"], xT, T)
                if "l0a" in ph:
                    for rg in range(T // 512):
                        emit_ln_fm(ctx, xT, 512, lnb, stride=T, t0=rg * 512,
                                   lnb_stride=T, lnb_t0=rg * 512)
                    emit_l0_attention(ctx, T, lnb, xT)
            with ctx.scope():
                p.tag("L0moe")
                lnb = ctx.sbuf("lnb", [128, 8 * T], BF16)
                ctx.sbuf("at_sb", [128, 8 * 512], BF16)
                ctx.sbuf("bwt_sb", [128, 4 * 128], BF16)
                ctx.sbuf("wrt_sb", [128, 8 * 8], BF16)
                ctx.sbuf("midall", [128, 4 * T], BF16)
                ctx.sbuf("grow", [1, 32], F32)
                ctx.sbuf("gcol", [64, 8], F32)
                ctx.sbuf("st_m_full", [1, T], F32)
                ctx.sbuf("st_r_full", [1, T], F32)
                at_b = ctx.bufs["at_sb"]
                wrt_b = ctx.bufs["wrt_sb"]
                _at_src = ctx.wsrc["AT"].rearrange("(c p) n -> p c n", p=128)
                _at_dst = at_b[:].rearrange("p (c n) -> p c n", c=8)
                p.load("moew", lambda e, a=_at_dst, s_=_at_src: e.dma_start(out=a, in_=s_),
                       queue=ctx.wqueue())
                for m in range(8):
                    p.load("moew", lambda e, m=m, wrt_b=wrt_b: e.dma_start(
                        out=wrt_b[:, m * 8:(m + 1) * 8],
                        in_=dram["WrT"][m * 128:(m + 1) * 128, :]), queue="gpsimd")
                if "l0f" in ph:
                    for rg in range(T // 512):
                        emit_ln_fm(ctx, xT, 512, lnb, stride=T, t0=rg * 512,
                                   lnb_stride=T, lnb_t0=rg * 512,
                                   m_out=ctx.bufs["st_m_full"][:, rg * 512:(rg + 1) * 512],
                                   r_out=ctx.bufs["st_r_full"][:, rg * 512:(rg + 1) * 512])
                    emit_l0_moe(ctx, T, lnb, xT)
                p.mark("moew")
                p.mark("bwt")
            with ctx.scope():
                p.tag("L1ret")
                if "cst" in ph or "l1a" in ph:
                    load_consts(["dt_t", "cqb", "ckb", "gnb"])
                ctx.sbuf("lnr", [128, 8 * 512], BF16)
                ctx.sbuf("qk_sb", [128, 16 * 256], BF16)
                ctx.sbuf("wv_sb", [128, 8 * 512], BF16)
                ctx.sbuf("wol1", [128, 16 * 128], BF16)
                ctx.sbuf("qraw", [128, 2 * 512], BF16)
                ctx.sbuf("qhat", [128, 2 * 512], BF16)
                ctx.sbuf("khat", [128, 2 * 512], BF16)
                ctx.sbuf("vtile", [128, 4 * 512], BF16)
                ctx.sbuf("obuf", [128, 4 * 512], BF16)
                ctx.sbuf("gb", [128, 512], BF16)
                ctx.sbuf("ofscr", [128, 512], BF16)
                ctx.sbuf("ofT", [128, 16 * 512], BF16)
                ctx.sbuf("qts", [128, 2 * 128], BF16)
                ctx.sbuf("kts", [128, 2 * 128], BF16)
                ctx.sbuf("ktok", [128, 256], BF16)
                ctx.sbuf("sqs2", [128, 512], BF16)
                ctx.sbuf("racc", [128, 2], F32)
                ctx.sbuf("Sf", [128, 8 * 512], F32)
                ctx.sbuf("Sb", [128, 8 * 512], BF16)
                ctx.sbuf("ropes", [128, 4 * 512], BF16)
                if "l1a" in ph:
                    emit_l1_retention(ctx, T, xT)
                for sl in ("l1qk", "l1v", "wol1", "ropes"):
                    p.mark(sl)
            if "spl" in ph or "l0a" in ph:
                for c in range(8):
                    p.store(lambda e, c=c, s=s, xT=xT: e.dma_start(
                        out=dram[f"xout_s{s}"][c * 128:(c + 1) * 128, :],
                        in_=xT[:, c * T:(c + 1) * T]))
        with ctx.scope():
            if "cst" in ph or "l1f" in ph:
                load_consts(["rperm", "lmask", "hsel", "qsind", "lngc", "lnbc",
                             "b1ac", "b1gc", "b2c"])
            xI = ctx.sbuf("xI", [128, 8 * TP], F32)
            ctx.sbuf("wbig", [128, 8 * 1024], BF16)
            ctx.sbuf("wbig2", [128, 8 * 1024], BF16)
            ctx.sbuf("lnr", [128, 8 * 512], BF16)
            ctx.sbuf("qsc", [128, 2 * 512], BF16)
            ctx.sbuf("tsc2", [128, 2 * 512], BF16)
            ctx.sbuf("rnorm2", [2, 2 * 512], BF16)
            ctx.sbuf("bnd", [128, 32], F32)

            def pads(eng, xI=xI):
                return eng.memset(xI[:].rearrange("p (c t) -> p c t", c=8)[:, :, 0:1], 0.0)

            p.comp("vector", pads, wait_stores=True)
            with ctx.scope():
                xor_ = ctx.sbuf("xoutr", [128, 8 * 512], F32)
                for rg in range(T // R):
                    t0 = rg * R
                    for c in range(8):
                        p.load(f"xo{c % 2}", lambda e, c=c, s=s, t0=t0, xor_=xor_: e.dma_start(
                            out=xor_[:, c * R:(c + 1) * R],
                            in_=dram[f"xout_s{s}"][c * 128:(c + 1) * 128, t0:t0 + R]),
                            queue="sync", gate_stores=True)
                    p.comp("vector", lambda e: e.engine_nop(), needs=("xo0", "xo1"))
                    lnr_b = ctx.bufs["lnr"]
                    emit_ln_fm(ctx, None, R, None,
                               xc_fn=lambda c, xor_=xor_: xor_[:, c * R:(c + 1) * R],
                               lc_fn=lambda c, t0=t0, xI=xI: xI[:, c * TP + 1 + t0: c * TP + 1 + t0 + R],
                               cast_fn=lambda c, lnr_b=lnr_b: lnr_b[:, c * R:(c + 1) * R])
                    p.mark("xo0")
                    p.mark("xo1")
            if "l1f" in ph:
                for li in range(6):
                    if s == 0 and li + 1 < 6:
                        precast([f"L1f_Wqkv_{li + 1}", f"L1f_Wo_{li + 1}",
                                 f"L1f_W1_{li + 1}", f"L1f_W2_{li + 1}"])
                    emit_l1ffn_layer(ctx, T, xI, li)
            p.tag("fin")
            ident = ctx.bufs["ident"]
            stage = ctx.bufs["tstage"][:, 0:1024]
            xor_ = ctx.sbuf("xoutr_fin", [128, 8 * 512], F32)
            p.mark("xo0")
            p.mark("xo1")
            for rg in range(T // R):
                t0 = rg * R
                for c in range(8):
                    p.load(f"xo{c % 2}", lambda e, c=c, s=s, t0=t0, xor_=xor_: e.dma_start(
                        out=xor_[:, c * R:(c + 1) * R],
                        in_=dram[f"xout_s{s}"][c * 128:(c + 1) * 128, t0:t0 + R]),
                        queue="sync", gate_stores=True)
                p.comp("vector", lambda e: e.engine_nop(), needs=("xo0", "xo1"))
                for c in range(8):
                    p.comp("vector", lambda e, c=c, t0=t0, xor_=xor_, xI=xI: e.tensor_add(
                        out=xor_[:, c * R:(c + 1) * R], in0=xor_[:, c * R:(c + 1) * R],
                        in1=xI[:, c * TP + 1 + t0: c * TP + 1 + t0 + R]))
                for tt in range(R // 128):
                    for g in range(2):
                        pb = ctx.bank()

                        def tr(eng, g=g, tt=tt, pb=pb, t0=t0, xor_=xor_):
                            last_i = None
                            for i in range(4):
                                c = g * 4 + i
                                last_i = eng.matmul(
                                    pb[:, i * 128:(i + 1) * 128],
                                    xor_[:, c * R + tt * 128: c * R + (tt + 1) * 128],
                                    ident[:], start=(i == 0), stop=(i == 3),
                                    is_transpose=True)
                            return last_i

                        p.comp("tensor", tr)

                        def ev(eng, g=g, pb=pb):
                            last_i = None
                            for i in range(4):
                                c = g * 4 + i
                                last_i = eng.tensor_copy(
                                    out=stage[:, c * 128:(c + 1) * 128],
                                    in_=pb[:, i * 128:(i + 1) * 128])
                            return last_i

                        p.comp("vector", ev, wait_stores=(g == 0))
                    p.store(lambda e, s=s, t0=t0, tt=tt: e.dma_start(
                        out=dram[f"y_s{s}"][t0 + tt * 128: t0 + (tt + 1) * 128, :],
                        in_=stage[:]))
                p.mark("xo0")
                p.mark("xo1")
    p.finish()
    p.emit()
    nc.finalize()
    return nc, es, {"const_vals": dict(ctx.const_vals)}



_DEP_OPCODES = ("DMACopy", "Matmult", "TensorTensor", "TensorCopy", "Memset",
                "Activation", "TensorScalarPtr", "TensorScalar", "TensorReduce",
                "ScalarTensorTensor", "Reciprocal", "Copy", "Select",
                "ActivationReduceCmd", "Iota", "MemSet", "InstActivation")


_P2_PLAN = None
_P2_LAST_PLAN = None


class Prog2(Prog):

    NDMASEMS = {"gpsimd": 58, "sync": 28}
    USE_CHAIN = False

    def __init__(self, nc, es):
        super().__init__(nc, es)
        self.esems = {e: [es.enter_context(nc.semaphore(f"es_{e}{i}")) for i in range(4)]
                      for e in ("tensor", "vector", "scalar")}
        self.dsems = {q: [es.enter_context(nc.semaphore(f"ds_{q}{i}")) for i in range(n)]
                      for q, n in self.NDMASEMS.items()}
        self.barriers = []

    def barrier(self):
        self.barriers.append(self._gseq())

    def load(self, slot, fn, queue="sync", gate_stores=False):
        self.steps[queue].append(("load", None, [], fn, None, self._gseq(),
                                  getattr(self, "_tag", None)))

    def store(self, fn, queue="sync"):
        self.nstores += 1
        self.steps[queue].append(("store", None, [], fn, None, self._gseq(),
                                  getattr(self, "_tag", None)))

    def comp(self, engine, fn, needs=(), wait_stores=False):
        self.steps[engine].append(("comp", None, [], fn, None, self._gseq(),
                                   getattr(self, "_tag", None)))

    def mark(self, slot):
        pass

    def gate_stores(self):
        pass

    def finish(self):
        self.steps["vector"].append(("finwait", None, [], None, None, self._gseq(), None))

    def _sem_for(self, skey):
        kind = skey[0]
        if kind == "eng":
            return self.esems[skey[1]][skey[2]]
        if kind == "dma":
            return self.dsems[skey[1]][skey[2]]
        raise KeyError(skey)

    def emit(self):
        global _P2_LAST_PLAN
        nc = self.nc
        import bass_rust
        plan = _P2_PLAN
        records = {e: [] for e in ENGINES}

        with nc.Block() as block:
            for ename in ENGINES:
                seq = self.steps[ename]
                if not seq:
                    continue
                if plan is not None:
                    pos = plan["pos"]
                    seq = sorted(seq, key=lambda st: pos.get(st[5], 10 ** 9))

                def make(seq=seq, ename=ename):
                    def _(eng):
                        cur_tag = None
                        cur_sid = None
                        for st in seq:
                            kind, _w, _sw, fn, _inc, gidx, tagv = st
                            if tagv != cur_tag:
                                if cur_tag is not None:
                                    nc.leave_named_scope(cur_tag, cur_sid, False)
                                if tagv is not None:
                                    cur_sid = nc.enter_named_scope(tagv, False)[0]
                                cur_tag = tagv
                            if plan is not None:
                                for skey, val in plan["waits"].get(gidx, ()):
                                    eng.wait_ge(self._sem_for(skey), val)
                            if kind == "finwait":
                                if plan is not None:
                                    for skey, val in plan["finwaits"]:
                                        eng.wait_ge(self._sem_for(skey), val)
                                else:
                                    eng.engine_nop()
                                records[ename].append({"kind": kind, "insts": [],
                                                       "engine": ename, "gidx": gidx})
                                continue
                            got = []
                            nc.push_inst_callback(lambda i, got=got: got.append(i))
                            try:
                                fn(eng)
                            finally:
                                nc.pop_inst_callback()
                            if plan is not None:
                                for j, skey, amt in plan["incs"].get(gidx, ()):
                                    bass_rust.BassInstruction(got[j]).then_inc(
                                        self._sem_for(skey), amt)
                            records[ename].append({"kind": kind, "insts": got,
                                                   "engine": ename, "gidx": gidx})
                        if cur_tag is not None:
                            nc.leave_named_scope(cur_tag, cur_sid, False)
                    return _

                getattr(block, ename)(make())

        if plan is not None:
            return

        flat = []
        for ename in ENGINES:
            for rec in records[ename]:
                if rec["kind"] == "finwait":
                    continue
                for j, mi in enumerate(rec["insts"]):
                    flat.append({"mi": mi, "engine": ename, "gidx": rec["gidx"],
                                 "j": j, "kind": rec["kind"]})
        flat.sort(key=lambda r: (r["gidx"], r["j"]))
        for qi, r in enumerate(flat):
            r["qi"] = qi

        psum_names = {f"pb{i}_set" for i in range(7)} | {"psbf_set"}
        import re as _re
        _sbuf_re = _re.compile(r"_u\d+_set$")

        def spans(args):
            out = []
            for a in args:
                try:
                    ms = a.memsetref
                    ap = a.ap
                    off = a.offset
                except Exception:
                    continue
                if ms is None:
                    continue
                if ms in psum_names:
                    out.append((ms, 0, 1, 4096))
                    continue
                if _sbuf_re.search(ms) and len(ap) >= 1 and ap[0][0] > 0:
                    pstride = ap[0][0]
                    ext = 1
                    for step, count in ap[1:]:
                        ext += abs(step) * (count - 1)
                    lo = off % pstride
                    out.append((ms, int(lo), int(lo + ext), 256))
                    continue
                ext = 1
                for step, count in ap:
                    ext += abs(step) * (count - 1)
                out.append((ms, int(off), int(off + ext), 65536))
            return out

        track = {}
        deps_sem = [()] * len(flat)
        deps_ord = [()] * len(flat)
        for r in flat:
            mi = r["mi"]
            reads = spans(mi.ins)
            writes = spans(mi.outs)
            if not reads and not writes:
                deps_sem[r["qi"]] = set()
                deps_ord[r["qi"]] = set()
                continue
            r["is_dma"] = type(mi).__name__ == "InstDMACopy"
            r["reads"] = reads
            r["writes"] = writes
            qi = r["qi"]
            mydeps = set()
            for (t, lo, hi, bz) in reads:
                tb = track.get(t)
                if tb:
                    for b in range(lo // bz, hi // bz + 1):
                        e = tb.get(b)
                        if e and e[0] is not None:
                            mydeps.add(e[0])
            for (t, lo, hi, bz) in writes:
                tb = track.get(t)
                if tb:
                    for b in range(lo // bz, hi // bz + 1):
                        e = tb.get(b)
                        if e:
                            if e[0] is not None:
                                mydeps.add(e[0])
                            mydeps.update(e[1])
            for (t, lo, hi, bz) in reads:
                tb = track.setdefault(t, {})
                for b in range(lo // bz, hi // bz + 1):
                    e = tb.get(b)
                    if e is None:
                        tb[b] = (None, {qi})
                    else:
                        e[1].add(qi)
            for (t, lo, hi, bz) in writes:
                tb = track.setdefault(t, {})
                for b in range(lo // bz, hi // bz + 1):
                    tb[b] = (qi, set())
            mydeps.discard(qi)
            deps_ord[qi] = set(mydeps)
            sd = mydeps
            if r["engine"] == "tensor":
                sd = {d for d in sd
                      if flat[d]["engine"] != "tensor" or flat[d].get("is_dma")}
            sd = {d for d in sd if flat[d]["gidx"] != r["gidx"]}
            deps_sem[qi] = sd

        by_step = {}
        for r in flat:
            by_step.setdefault(r["gidx"], []).append(r)
        step_list = sorted(by_step)
        qi2g = {r["qi"]: r["gidx"] for r in flat}

        def icost(r):
            nm = type(r["mi"]).__name__
            outs = r["mi"].outs
            fe = 0
            nbytes = 0
            for a in outs:
                try:
                    c = 1
                    ap = a.ap
                    for k in range(1, len(ap)):
                        c *= ap[k][1]
                    fe = max(fe, c)
                    dt_ = getattr(a, "dtype", None)
                    if dt_ is None:
                        dt_ = a.tensor.dtype
                    nb = c * ap[0][1] * mybir.dt.size(dt_)
                    nbytes = max(nbytes, nb)
                except Exception:
                    pass
            if nm == "InstMatmult":
                fp32 = False
                try:
                    a0 = r["mi"].ins[0]
                    dt_ = getattr(a0, "dtype", None)
                    if dt_ is None:
                        dt_ = a0.tensor.dtype
                    fp32 = dt_ == mybir.dt.float32
                except Exception:
                    pass
                return 90 + fe * (1.75 if fp32 else 0.45)
            if nm == "InstDMACopy":
                inb = 0
                for a in r["mi"].ins:
                    try:
                        c = 1
                        ap = a.ap
                        for k in range(1, len(ap)):
                            c *= ap[k][1]
                        dt_ = getattr(a, "dtype", None)
                        if dt_ is None:
                            dt_ = a.tensor.dtype
                        inb = max(inb, c * ap[0][1] * mybir.dt.size(dt_))
                    except Exception:
                        pass
                return 1200 + max(nbytes, inb) * 0.008
            if r["engine"] == "scalar":
                return 150 + fe * 0.82
            return 120 + fe * 0.62

        step_deps = {g: set() for g in step_list}
        step_cost = {g: 0.0 for g in step_list}
        for r in flat:
            g = r["gidx"]
            step_cost[g] += icost(r)
            for d in deps_ord[r["qi"]]:
                dg = qi2g[d]
                if dg != g:
                    step_deps[g].add(dg)
        dram_only = set()
        for g, rs in by_step.items():
            ok = len(rs) > 0
            for r in rs:
                wn = [t for t, *_ in (r.get("writes") or [])]
                if not (r.get("is_dma") and wn and all(t.endswith("_bf_set") for t in wn)):
                    ok = False
                    break
            if ok:
                dram_only.add(g)
        barriers = sorted(self.barriers)
        bmap = []
        bi = 0
        prev_cut = []
        cur_seg = []
        seg_of = {}
        for g in step_list:
            while bi < len(barriers) and barriers[bi] <= g:
                if cur_seg:
                    prev_cut = cur_seg
                    cur_seg = []
                bi += 1
                bmap.append(g)
            seg_of[g] = bi
            if g not in dram_only:
                cur_seg.append(g)
            if prev_cut and g not in dram_only:
                step_deps[g].update(prev_cut)

        step_eng = {g: by_step[g][0]["engine"] for g in step_list}

        import heapq
        indeg = {g: 0 for g in step_list}
        succ = {g: [] for g in step_list}
        for g, ds in step_deps.items():
            for d in ds:
                succ[d].append(g)
                indeg[g] += 1
        XLAT0 = 450.0
        cp = {}
        for g in reversed(step_list):
            cp[g] = step_cost[g] + max(
                (cp[s] + (0.0 if step_eng[s] == step_eng[g] else XLAT0)
                 for s in succ[g]), default=0.0)
        XLAT = 450.0
        fin = {}
        clocks = {e: 0.0 for e in ENGINES}
        ready = {e: [] for e in ENGINES}
        for g in step_list:
            if indeg[g] == 0:
                heapq.heappush(ready[e_ := step_eng[g]], (-cp[g], g, 0.0))
        order = []
        nsched = 0

        def pick(e):
            h = ready[e]
            tmp = []
            found = None
            for _ in range(24):
                if not h:
                    break
                it = heapq.heappop(h)
                if it[2] <= clocks[e]:
                    found = it
                    break
                tmp.append(it)
            if found is None and tmp:
                found = min(tmp, key=lambda it: (it[2], it[0]))
                tmp.remove(found)
            for it in tmp:
                heapq.heappush(h, it)
            return found

        while nsched < len(step_list):
            best = None
            for e in ENGINES:
                if not ready[e]:
                    continue
                negcp, g, est = ready[e][0]
                start = max(clocks[e], est)
                key = (start, negcp, g)
                if best is None or key < best[0]:
                    best = (key, e)
            e = best[1]
            it = pick(e)
            negcp, g, est = it
            start = max(clocks[e], est)
            f = start + step_cost[g]
            clocks[e] = f
            fin[g] = f
            order.append(g)
            nsched += 1
            for s2 in succ[g]:
                indeg[s2] -= 1
                if indeg[s2] == 0:
                    e2 = step_eng[s2]
                    est2 = max((fin[d] + (0.0 if step_eng[d] == e2 else XLAT)
                                for d in step_deps[s2] if d in fin), default=0.0)
                    heapq.heappush(ready[e2], (-cp[s2], s2, est2))
        pos = {g: i for i, g in enumerate(order)}
        self._sched_span = max(clocks.values())

        flat.sort(key=lambda r: (pos[r["gidx"]], r["j"]))
        eng_n = {"tensor": 0, "vector": 0, "scalar": 0}
        key_slot = {}
        slot_count = {q: [0] * n for q, n in self.NDMASEMS.items()}
        slot_free = {q: list(range(n)) for q, n in self.NDMASEMS.items()}
        phase_keys = []
        inc_plan = {}
        cur_seg_id = None
        for r in flat:
            g = r["gidx"]
            if seg_of[g] != cur_seg_id:
                for k_ in phase_keys:
                    q_, s_ = key_slot[k_]
                    slot_free[q_].append(s_)
                phase_keys = []
                cur_seg_id = seg_of[g]
            r["inc"] = None
            if r.get("is_dma"):
                side = r["reads"] if r["kind"] == "store" else r["writes"]
                key = side[0][:3] if side else ("?", 0, 0)
                queue = r["engine"] if r["engine"] in ("gpsimd", "sync") else "sync"
                key = (queue,) + key
                if key not in set(phase_keys):
                    assert slot_free[queue], f"out of DMA sem slots ({queue})"
                    key_slot[key] = (queue, slot_free[queue].pop(0))
                    phase_keys.append(key)
                q_, slot = key_slot[key]
                slot_count[q_][slot] += 1
                skey = ("dma", q_, slot)
                r["inc"] = (skey, 16 * slot_count[q_][slot])
                inc_plan.setdefault(g, []).append((r["j"], skey, 16))
        needed = [False] * len(flat)
        qi_r = {r["qi"]: r for r in flat}
        for r in flat:
            for d in deps_sem[r["qi"]]:
                needed[d] = True
        for r in flat:
            if r.get("is_dma"):
                continue
            if needed[r["qi"]]:
                e = r["engine"]
                if e in eng_n:
                    n = eng_n[e]
                    skey = ("eng", e, n % 4)
                    eng_n[e] = n + 1
                    r["inc"] = (skey, n // 4 + 1)
                    inc_plan.setdefault(r["gidx"], []).append((r["j"], skey, 1))

        step_waits = {}
        waited = {e: {} for e in ENGINES}
        nwaits = 0
        cur = {}
        snap = None
        absorbed = set(ENGINES)
        seen_seg = None
        for g in order:
            rs = by_step[g]
            if seg_of[g] != seen_seg:
                if seen_seg is not None:
                    snap = dict(cur)
                    absorbed = set()
                seen_seg = seg_of[g]
            ename = rs[0]["engine"]
            tgt = {}
            if snap is not None and ename not in absorbed:
                absorbed.add(ename)
                tgt.update(snap)
            for r in rs:
                for d in deps_sem[r["qi"]]:
                    inc = qi_r[d].get("inc")
                    if inc is None:
                        continue
                    skey, val = inc
                    if skey not in tgt or tgt[skey] < val:
                        tgt[skey] = val
            for r in rs:
                inc = r.get("inc")
                if inc is not None:
                    skey, val = inc
                    if cur.get(skey, -1) < val:
                        cur[skey] = val
            wq = waited[ename]
            lst = []
            for skey, val in sorted(tgt.items()):
                if wq.get(skey, -1) >= val:
                    continue
                wq[skey] = val
                lst.append((skey, val))
                nwaits += 1
            if lst:
                step_waits[g] = lst
        finwaits = [(("dma", q, i), 16 * c)
                    for q, counts in slot_count.items()
                    for i, c in enumerate(counts) if c > 0]
        eng_load = {}
        for g in step_list:
            eng_load[step_eng[g]] = eng_load.get(step_eng[g], 0.0) + step_cost[g]
        _P2_LAST_PLAN = {"waits": step_waits, "incs": inc_plan, "finwaits": finwaits,
                         "pos": pos,
                         "stats": dict(n=len(flat), nwaits=nwaits, eng_n=eng_n,
                                       sched_span_us=self._sched_span / 1000.0,
                                       cp_us=max(cp.values()) / 1000.0,
                                       eng_load_us={k: round(v / 1000.0, 1)
                                                    for k, v in eng_load.items()})}


def build_twopass(builder):
    global _P2_PLAN
    _P2_PLAN = None
    builder()
    plan = _P2_LAST_PLAN
    _P2_PLAN = plan
    try:
        out = builder()
    finally:
        _P2_PLAN = None
    return out
